# version 20
# baseline (speedup 1.0000x reference)
import numpy as np
from contextlib import ExitStack

import concourse.bass as bass
import concourse.mybir as mybir
from concourse.bass_utils import run_bass_kernel_spmd

F32 = mybir.dt.float32
F16 = mybir.dt.float16
AF = mybir.ActivationFunctionType
ALU = mybir.AluOpType
AX = mybir.AxisListType

NB = 2
N = 2048
FD = 1024
H = 16
LH = 512
HHID = 4096
NT = N // 128
CUT_W = float(np.float32(-8.733654475))
LN2048 = float(np.log(np.float64(2048.0)))
CONST_R = float(np.float32(0.1) * np.log(np.float32(8.0)))
COOR_ITERS = 50
SEARCH_ITERS = 46

ENGS = ("sync", "gpsimd", "tensor", "vector", "scalar")


class Plan:
    def __init__(self, nc, stack):
        self.nc = nc
        self.stack = stack
        self.seq = {k: [] for k in ENGS}
        self.sems = {}
        self.semv = {}
        self.waited = {k: {} for k in ENGS}

    def sem(self, name):
        if name not in self.sems:
            self.sems[name] = self.stack.enter_context(self.nc.semaphore(name))
            self.semv[name] = 0
        return self.sems[name]

    DMA_SEMS = ("DG", "DX0", "DX1", "XR0", "XR1", "DS0", "DS1",
                "DO0", "DO1", "DW", "DH1", "DH2",
                "DC", "DB", "WQ", "WK", "WV", "WO", "DL", "DH0")

    def emit(self, eng, fn, wait=(), inc=None, n=1):
        if inc is not None:
            self.sem(inc)
        is_dma = inc in self.DMA_SEMS
        w = []
        for tok in wait:
            if tok is None:
                continue
            s, v = tok
            self.sem(s)
            if self.waited[eng].get(s, 0) < v or (is_dma and not w):
                self.waited[eng][s] = max(self.waited[eng].get(s, 0), v)
                w.append((s, v))
        if is_dma and not w:
            self.sem("Z")
            w.append(("Z", 1))
        tok = None
        if inc is not None:
            self.semv[inc] += n
            tok = (inc, self.semv[inc])
        sems = self.sems

        def run(e, w=tuple(w), fn=fn, inc=inc, n=n):
            for s, v in w:
                e.wait_ge(sems[s], v)
            instr = fn(e)
            if inc is not None:
                instr.then_inc(sems[inc], n)

        self.seq[eng].append(run)
        return tok

    def wait(self, eng, *toks):
        w = []
        for tok in toks:
            if tok is None:
                continue
            s, v = tok
            self.sem(s)
            if self.waited[eng].get(s, 0) < v:
                self.waited[eng][s] = v
                w.append((s, v))
        if not w:
            return
        sems = self.sems

        def run(e, w=tuple(w)):
            for s, v in w:
                e.wait_ge(sems[s], v)

        self.seq[eng].append(run)


def build_program(debug=False):
    stack = ExitStack()
    nc = bass.Bass("TRN2", target_bir_lowering=False,
                   detect_race_conditions=False)
    p = Plan(nc, stack)

    dx = nc.dram_tensor("x", [NB, N, FD], F32, kind="ExternalInput").ap()
    dW = {nm: nc.dram_tensor(nm, shp, F32, kind="ExternalInput").ap()
          for nm, shp in [("Wq", [FD, FD]), ("Wk", [FD, FD]), ("Wv", [FD, FD]),
                          ("Wo", [FD, FD]), ("w1l", [FD, LH]), ("w2l", [LH, FD]),
                          ("w1h", [FD, HHID]), ("w2h", [HHID, FD])]}
    dB = {nm: nc.dram_tensor(nm, [ln_], F32, kind="ExternalInput").ap()
          for nm, ln_ in [("bq", FD), ("bk", FD), ("bv", FD), ("bo", FD),
                          ("b1l", LH), ("b2l", FD), ("b1h", HHID), ("b2h", FD),
                          ("gl", FD), ("gh", FD)]}
    dm = nc.dram_tensor("mh", [H], F32, kind="ExternalInput").ap()
    drt = nc.dram_tensor("rtb", [128, FD], F32, kind="ExternalInput").ap()
    dc16 = nc.dram_tensor("c16", [128, 512], F32, kind="ExternalInput").ap()
    dhmT = nc.dram_tensor("hmT", [16, 8 * 128], F32, kind="ExternalInput").ap()
    dx1 = nc.dram_tensor("x1s", [NB, N, FD], F32,
                         kind="ExternalOutput" if debug else "Internal").ap()
    dout = nc.dram_tensor("out", [NB, N, FD], F32, kind="ExternalOutput").ap()
    ddbg = None
    ddbg2 = None
    if debug:
        ddbg = nc.dram_tensor("dbg", [NB, 8, 128, NT], F32, kind="ExternalOutput").ap()
        ddbg2 = nc.dram_tensor("dbg2", [6, 128, FD], F32, kind="ExternalOutput").ap()

    def sbuf(name, shape, dt):
        return stack.enter_context(nc.sbuf_tensor(name, shape, dt))

    c16 = sbuf("c16s", [128, 512], F16)
    idF32 = sbuf("idF32", [128, 128], F32)
    id16 = c16[:, 0:128]
    tri16 = c16[:, 128:256]
    ones128 = c16[:, 256:384]
    ones_row = c16[0:1, 256:384]
    ones_rows = {0: c16[0:1, 256:384], 32: c16[32:33, 256:384], 64: c16[64:65, 256:384]}
    bq_c = sbuf("bq_c", [128, 8], F32)
    b1l_c = sbuf("b1l_c", [128, 4], F32)
    b1h_c = sbuf("b1h_c", [128, 32], F32)
    gl_c = sbuf("gl_c", [128, 8], F32)
    gh_c = sbuf("gh_c", [128, 8], F32)
    brow_a = sbuf("brow_a", [65, FD], F16)
    brow_b = sbuf("brow_b", [65, FD], F16)
    brow = {"bk": brow_a[0:1, :], "bv": brow_a[32:33, :], "bo": brow_a[64:65, :],
            "b2l": brow_b[0:1, :], "b2h": brow_b[32:33, :]}
    ones_for = {"bk": ones_rows[0], "bv": ones_rows[32], "bo": ones_rows[64],
                "b2l": ones_rows[0], "b2h": ones_rows[32]}
    m16 = sbuf("m16", [16, 1], F32)
    rscale = sbuf("rscale", [16, 1], F32)
    sumrt = sbuf("sumrt", [128, 1], F32)
    junkS = sbuf("junkS", [128, FD], F16)

    ST = [{nm: sbuf(f"st{b}_" + nm, [128, NT], F32)
           for nm in ("sum3", "sumsq3", "xr", "m3", "alpha", "s_t", "sp", "bp",
                      "mask", "sum4", "sumsq4")} for b in range(NB)]
    RT = {nm: sbuf("rt_" + nm, [128, NT], F32)
          for nm in ("t_t", "e_t", "w_t", "key", "eq", "gtm", "cum",
                     "colsum", "prefx")}
    eq16 = sbuf("rt_eq16", [128, NT], F16)
    I32 = mybir.dt.int32
    CL = {nm: sbuf("cl_" + nm, [128, 1], F32)
          for nm in ("t1", "t2", "tau", "quota")}
    TL = {nm: sbuf("tl_" + nm, [128, 1], F32) for nm in ("r1", "r2", "r3", "r4", "r5", "r6")}
    T4 = {nm: sbuf("t4_" + nm, [128, 1], F32) for nm in ("m4", "i4", "v4", "w4")}

    xin = [sbuf(f"xin{i}", [128, FD], F32) for i in range(2)]
    xrs = [sbuf(f"xrs{i}", [128, FD], F32) for i in range(2)]
    f_sb = [sbuf(f"f_sb{i}", [128, FD], F32) for i in range(2)]
    hvm = [sbuf(f"hvm{i}", [128, FD], F32) for i in range(2)]

    pool16 = stack.enter_context(nc.psum_tensor("pool16", [128, 1024], F16))
    poolA = stack.enter_context(nc.psum_tensor("poolA", [128, 1024], F32))
    poolB = stack.enter_context(nc.psum_tensor("poolB", [128, 1024], F32))
    poolK = stack.enter_context(nc.psum_tensor("poolK", [128, 1024], F32))
    poolD = stack.enter_context(nc.psum_tensor("poolD", [128, 512], F32))

    def mk_routing_bufs(scope, tag, side=None):
        g = {}
        for nm in ("s16", "sp16", "bp16", "t16", "e16", "w16", "km16"):
            g[nm] = scope.enter_context(nc.sbuf_tensor(f"rb{tag}_{nm}", [32, 128], F32, side=side))
        g["key16"] = scope.enter_context(nc.sbuf_tensor(f"rb{tag}_key16", [32, 128], F32, side=side))
        g["gti16"] = scope.enter_context(nc.sbuf_tensor(f"rb{tag}_gti16", [32, 128], I32, side=side))
        g["ii16"] = g["gti16"]
        scal = scope.enter_context(nc.sbuf_tensor(f"rb{tag}_scal", [32, 24], F32, side=side))
        names = ("a16", "a10s", "nmxs", "nsmaxs", "los", "his", "mids", "taus",
                 "cnts", "mgts", "qts", "redc", "tr1", "tr2", "trb", "lns", "tmps")
        for j, nm in enumerate(names):
            g[nm] = scal[:, j:j + 1]
        iscal = scope.enter_context(nc.sbuf_tensor(f"rb{tag}_iscal", [32, 2], I32, side=side))
        g["pr16"] = iscal[:, 0:1]
        g["np16"] = iscal[:, 1:2]
        g["rowbuf"] = g["t16"][0:1, :]
        g["qt1"] = scope.enter_context(nc.sbuf_tensor(f"rb{tag}_qt1", [32, 32], F32, side=side))
        g["qt2"] = scope.enter_context(nc.sbuf_tensor(f"rb{tag}_qt2", [32, 32], F32, side=side))
        return g


    RB0 = mk_routing_bufs(stack, "0", side="right")

    ab1 = ExitStack()
    ab2 = ExitStack()
    hmT16 = ab1.enter_context(nc.sbuf_tensor("hmT16", [16, 8 * 128], F16))
    qinv = ab1.enter_context(nc.sbuf_tensor("qinv", [16, 512], F32))
    qinv16 = ab1.enter_context(nc.sbuf_tensor("qinv16", [16, 512], F16))
    kstat = ab1.enter_context(nc.sbuf_tensor("kstat", [128, 16], F32))
    W16 = {}
    for nm in ("Wq", "Wk", "Wv"):
        W16[nm] = ab1.enter_context(nc.sbuf_tensor(nm + "_16", [128, 8 * 1024], F16))
    W16["Wo"] = ab2.enter_context(nc.sbuf_tensor("Wo_16", [128, 8 * 1024], F16, side="right"))
    xnT = ab1.enter_context(nc.sbuf_tensor("xnT", [128, 8 * 512], F16))
    xn16_0 = ab1.enter_context(nc.sbuf_tensor("xn16_0", [128, FD], F16))
    xn16_1 = ab1.enter_context(nc.sbuf_tensor("xn16_1", [128, FD], F16))
    xn16 = [xn16_0, xn16_1]
    kv_acc = ab1.enter_context(nc.sbuf_tensor("kv_acc", [128, 8 * 128], F32))
    qf32 = ab1.enter_context(nc.sbuf_tensor("qf32", [128, 8 * 512], F32))
    qsq16_0 = ab1.enter_context(nc.sbuf_tensor("qsq16_0", [128, 512], F16))
    qsq16_1 = ab1.enter_context(nc.sbuf_tensor("qsq16_1", [128, 512], F16))
    qsq16 = [qsq16_0, qsq16_1]
    k16_0 = ab1.enter_context(nc.sbuf_tensor("k16_0", [128, FD], F16))
    k16 = [k16_0, k16_0]
    v16_0 = ab1.enter_context(nc.sbuf_tensor("v16_0", [128, FD], F16))
    v16 = [v16_0, v16_0]
    ksq = ab1.enter_context(nc.sbuf_tensor("ksq", [128, FD], F16))
    t1b = ab2.enter_context(nc.sbuf_tensor("t1b", [128, FD], F32, side="right"))
    rtb = ab2.enter_context(nc.sbuf_tensor("rtb_s", [128, FD], F16, side="right"))
    q16 = ab2.enter_context(nc.sbuf_tensor("q16", [128, 8 * 2048], F16, side="right"))
    kv16 = ab2.enter_context(nc.sbuf_tensor("kv16", [128, 8 * 128], F16, side="right"))
    attn16 = ab2.enter_context(nc.sbuf_tensor("attn16", [128, 8 * 512], F16, side="right"))
    ao_sb0 = ab2.enter_context(nc.sbuf_tensor("ao_sb0", [128, FD], F16, side="right"))
    ao_sb = [ao_sb0, ao_sb0]

    def g_dma(dst, src, wait=(), slow=False, inc="DG"):
        return p.emit("gpsimd",
                      lambda e, d=dst, s=src, sl=slow: e.dma_start(
                          out=d, in_=s, allow_slow_non_contiguous=sl),
                      wait=wait, inc=inc, n=16)

    def v_dma(dst, src, wait=(), slow=False, inc="DB"):
        return p.emit("scalar",
                      lambda e, d=dst, s=src, sl=slow: e.dma_start(
                          out=d, in_=s, allow_slow_non_contiguous=sl),
                      wait=wait, inc=inc, n=16)

    def s_dma(dst, src, wait=(), inc="DX0"):
        return p.emit("sync", lambda e, d=dst, s=src: e.dma_start(out=d, in_=s),
                      wait=wait, inc=inc, n=16)

    p.emit("vector", lambda e: e.memset(TL["r1"][:], 0.0), inc="Z")
    g_dma(c16[:], dc16[:], inc="DC")
    g_dma(idF32[:], dc16[:, 0:128], inc="DC")
    g_dma(hmT16[:], dhmT[:], inc="DC")
    tok_rtb = g_dma(rtb[:], drt[:], inc="DC")
    tok_c = tok_rtb
    v_dma(bq_c[:], dB["bq"].rearrange("(m p) -> p m", p=128), slow=True)
    v_dma(b1l_c[:], dB["b1l"].rearrange("(m p) -> p m", p=128), slow=True)
    v_dma(b1h_c[:], dB["b1h"].rearrange("(m p) -> p m", p=128), slow=True)
    v_dma(gl_c[:], dB["gl"].rearrange("(m p) -> p m", p=128), slow=True)
    v_dma(gh_c[:], dB["gh"].rearrange("(m p) -> p m", p=128), slow=True)
    for nm in ("bk", "bv", "bo", "b2l", "b2h"):
        g_dma(brow[nm], dB[nm].rearrange("(o f) -> o f", o=1), inc="DB")
    tok_b0 = v_dma(m16[:], dm.rearrange("(h o) -> h o", o=1), slow=True)
    tok_b = ("DB", p.semv["DB"])
    tok_m = tok_b
    wsem = {"Wq": "WQ", "Wk": "WK", "Wv": "WV", "Wo": "WO"}
    tok_w = {}
    for nm in ("Wq", "Wk", "Wv", "Wo"):
        for k in range(8):
            tok_w[nm] = g_dma(W16[nm][:, 1024 * k:1024 * (k + 1)],
                              dW[nm][128 * k:128 * (k + 1), :], inc=wsem[nm])

    t = p.emit("scalar", lambda e: e.activation(CL["t1"][0:16, :], m16[:], AF.Exp, scale=-1.0),
               wait=[tok_m], inc="S")
    t = p.emit("vector", lambda e: e.tensor_scalar_add(CL["t2"][0:16, :], CL["t1"][0:16, :], 1.0),
               wait=[t], inc="V")
    t = p.emit("vector", lambda e: e.reciprocal(CL["t1"][0:16, :], CL["t2"][0:16, :]),
               wait=[t], inc="V")
    t = p.emit("scalar", lambda e: e.activation(CL["t2"][0:16, :], CL["t1"][0:16, :],
                                                AF.Exp, scale=LN2048),
               wait=[t], inc="S")
    tok_rs = p.emit("vector", lambda e: e.reciprocal(rscale[:], CL["t2"][0:16, :]),
                    wait=[t], inc="V")
    tok_srt = p.emit("scalar", lambda e: e.activation(junkS[:], rtb[:], AF.Copy, accum_out=sumrt[:]),
                     wait=[tok_rtb], inc="S")

    war = {}

    def ln_stats_inv(src_tile, wait_toks):
        wts = list(wait_toks) + [war.get("lnstats")]
        t_s = p.emit("scalar", lambda e: e.activation(junkS[:], src_tile[:], AF.Square,
                                                      accum_out=TL["r1"][:]),
                     wait=wts, inc="S")
        t_v = p.emit("scalar", lambda e: e.activation(junkS[:], src_tile[:], AF.Copy,
                                                      accum_out=TL["r2"][:]),
                     wait=[t_s], inc="S")
        t_m = p.emit("vector", lambda e: e.tensor_scalar_mul(TL["r3"][:], TL["r2"][:], 1.0 / FD),
                     wait=[t_v], inc="V")
        t_a = p.emit("vector", lambda e: e.tensor_scalar(TL["r4"][:], TL["r3"][:], TL["r3"][:],
                                                         -1e-5, ALU.mult, ALU.add),
                     wait=[t_m], inc="V")
        t_b = p.emit("vector", lambda e: e.scalar_tensor_tensor(TL["r4"][:], TL["r1"][:], 1.0 / FD,
                                                                TL["r4"][:], ALU.mult, ALU.subtract),
                     wait=[t_a, t_s], inc="V")
        t_l = p.emit("scalar", lambda e: e.activation(TL["r5"][:], TL["r4"][:], AF.Ln),
                     wait=[t_b], inc="S")
        t_i = p.emit("scalar", lambda e: e.activation(TL["r6"][:], TL["r5"][:], AF.Exp, scale=-0.5),
                     wait=[t_l], inc="S")
        return t_i

    def phase_A(b, pump):
        tok_kv_stopped = None
        def xload(tc):
            buf = tc % 2
            war[("ax", tc)] = s_dma(xin[buf][:], dx[b, 128 * tc:128 * (tc + 1), :],
                                    wait=[war.get(("xin", buf))], inc=f"DX{buf}")

        xload(0)
        xload(1)
        for st4 in range(4):
            ev_last = None
            for i in range(4):
                tc = 4 * st4 + i
                buf = tc % 2
                tok_x = war.get(("ax", tc))
                t_inv = ln_stats_inv(xin[buf], [tok_x])
                t_xn = p.emit("vector",
                              lambda e, buf=buf: e.tensor_scalar(
                                  xn16[buf][:], xin[buf][:], TL["r3"][:], TL["r6"][:],
                                  ALU.subtract, ALU.mult),
                              wait=[t_inv, war.get(("xn16", buf))], inc="V")
                war[("xin", buf)] = t_xn
                war["lnstats"] = t_xn
                if tc + 2 < NT:
                    xload(tc + 2)
                tt = None
                for kc in range(8):
                    tt = p.emit("tensor",
                                lambda e, buf=buf, kc=kc: e.transpose(
                                    pool16[:, 128 * kc:128 * (kc + 1)],
                                    xn16[buf][:, 128 * kc:128 * (kc + 1)], id16),
                                wait=[t_xn, war.get("pool16"), tok_c], inc="T")
                war[("xn16", buf)] = tt
                ev = p.emit("scalar",
                            lambda e, i=i: e.copy(
                                xnT[:].rearrange("p (k t) -> p k t", k=8)[:, :, 128 * i:128 * (i + 1)],
                                pool16[:].rearrange("p (k t) -> p k t", k=8)),
                            wait=[tt, war.get("xnT")], inc="S")
                war["pool16"] = ev
                ev_last = ev
            tok_xnT = ev_last

            tok_q_ev = [None] * 8
            tok_hs = None
            for mq in range(8):
                half = mq % 2
                psl = poolA[:, 512 * half:512 * (half + 1)]
                tk = None
                for kc in range(8):
                    tk = p.emit("tensor",
                                lambda e, mq=mq, kc=kc, psl=psl: e.matmul(
                                    psl,
                                    W16["Wq"][:, 1024 * kc + 128 * mq:1024 * kc + 128 * (mq + 1)],
                                    xnT[:, 512 * kc:512 * (kc + 1)],
                                    start=(kc == 0), stop=(kc == 7)),
                                wait=[tok_xnT, war.get(("poolA", half)), tok_w["Wq"]], inc="T")
                ev = p.emit("scalar",
                            lambda e, mq=mq, psl=psl: e.copy(
                                qf32[:, 512 * mq:512 * (mq + 1)], psl),
                            wait=[tk], inc="S")
                war[("poolA", half)] = ev
                tok_q_ev[mq] = ev
                sq = p.emit("scalar",
                            lambda e, mq=mq: e.activation(
                                qsq16[mq % 2][:], qf32[:, 512 * mq:512 * (mq + 1)], AF.Square,
                                bias=bq_c[:, mq:mq + 1]),
                            wait=[ev, tok_b, war.get(("qsq", mq % 2))], inc="S")
                tok_hs = p.emit("tensor",
                                lambda e, mq=mq: e.matmul(
                                    poolD[0:16, 0:512], c16[:, 384 + 16 * mq:384 + 16 * (mq + 1)],
                                    qsq16[mq % 2][:], start=(mq == 0), stop=(mq == 7)),
                                wait=[sq] + ([war.get("poolD")] if mq == 0 else []), inc="T")
                war[("qsq", mq % 2)] = tok_hs
            t0 = p.emit("vector", lambda e: e.tensor_scalar_max(qinv[:], poolD[0:16, 0:512], 1e-24),
                        wait=[tok_hs], inc="V")
            war["poolD"] = t0
            t1_ = p.emit("scalar", lambda e: e.activation(qinv[:], qinv[:], AF.Ln),
                         wait=[t0], inc="S")
            t2_ = p.emit("scalar", lambda e: e.activation(qinv[:], qinv[:], AF.Exp, scale=-0.5),
                         wait=[t1_], inc="S")
            t3_ = p.emit("vector", lambda e: e.tensor_scalar(qinv16[:], qinv[:], rscale[:], None,
                                                             ALU.mult),
                         wait=[t2_, tok_rs, war.get("qinv16")], inc="V")
            tok_ap = [t3_, t3_]
            last_bc = None
            for mq in range(8):
                half = mq % 2
                psl = poolA[:, 512 * half:512 * (half + 1)]
                bc = p.emit("tensor",
                            lambda e, mq=mq, psl=psl: e.matmul(
                                psl, hmT16[:, 128 * mq:128 * (mq + 1)], qinv16[:],
                                start=True, stop=True),
                            wait=[tok_ap[half], tok_q_ev[7], t3_], inc="T")
                ap_ = p.emit("vector",
                             lambda e, mq=mq, st4=st4, psl=psl: e.scalar_tensor_tensor(
                                 q16[:, 2048 * mq + 512 * st4:2048 * mq + 512 * (st4 + 1)],
                                 qf32[:, 512 * mq:512 * (mq + 1)], bq_c[:, mq:mq + 1], psl,
                                 ALU.add, ALU.mult),
                             wait=[bc], inc="V")
                tok_ap[half] = ap_
                last_bc = bc
            war[("poolA", 0)] = tok_ap[0]
            war[("poolA", 1)] = tok_ap[1]
            war["qinv16"] = last_bc
            tok_qdone = tok_ap

            xnT3 = xnT[:].rearrange("p (k t) -> p k t", k=8)
            tkv = None
            for i in range(4):
                tc = 4 * st4 + i
                buf = tc % 2
                tk = None
                for kc in range(8):
                    for nh in range(2):
                        tk = p.emit("tensor",
                                    lambda e, kc=kc, nh=nh, i=i: e.matmul(
                                        poolA[:, 512 * nh:512 * (nh + 1)],
                                        xnT3[:, kc, 128 * i:128 * (i + 1)],
                                        W16["Wk"][:, 1024 * kc + 512 * nh:1024 * kc + 512 * (nh + 1)],
                                        start=(kc == 0), stop=False),
                                    wait=[tok_qdone[0], tok_qdone[1], tok_w["Wk"], tok_b,
                                          war.get(("poolA", nh))] if kc == 0 else [], inc="T")
                for nh in range(2):
                    tk = p.emit("tensor",
                                lambda e, nh=nh: e.matmul(
                                    poolA[:, 512 * nh:512 * (nh + 1)], ones_for["bk"],
                                    brow["bk"][:, 512 * nh:512 * (nh + 1)],
                                    start=False, stop=True),
                                wait=(), inc="T")
                sq = p.emit("scalar", lambda e: e.activation(ksq[:], poolA[:], AF.Square),
                            wait=[tk, war.get("ksq")], inc="S")
                rd = p.emit("vector", lambda e: e.tensor_reduce(
                    kstat[:], ksq[:].rearrange("p (h d) -> p h d", h=16), AX.X, ALU.add),
                            wait=[sq, war.get("kstat")], inc="V")
                war["ksq"] = rd
                mx_ = p.emit("vector", lambda e: e.tensor_scalar_max(kstat[:], kstat[:], 1e-24),
                             wait=[rd], inc="V")
                l1 = p.emit("scalar", lambda e: e.activation(kstat[:], kstat[:], AF.Ln),
                            wait=[mx_], inc="S")
                l2 = p.emit("scalar", lambda e: e.activation(kstat[:], kstat[:], AF.Exp, scale=-0.5),
                            wait=[l1], inc="S")
                ap_ = p.emit("vector",
                             lambda e, buf=buf: e.tensor_tensor(
                                 k16[buf][:].rearrange("p (h d) -> p h d", h=16),
                                 poolA[:].rearrange("p (h d) -> p h d", h=16),
                                 kstat[:].rearrange("p (h o) -> p h o", o=1).broadcast_to((128, 16, 64)),
                                 ALU.mult),
                             wait=[l2, war.get(("k16", 0))], inc="V")
                war["kstat"] = ap_
                war[("poolA", 0)] = ap_
                war[("poolA", 1)] = ap_
                tv = None
                for kc in range(8):
                    for nh in range(2):
                        tv = p.emit("tensor",
                                    lambda e, kc=kc, nh=nh, i=i: e.matmul(
                                        poolB[:, 512 * nh:512 * (nh + 1)],
                                        xnT3[:, kc, 128 * i:128 * (i + 1)],
                                        W16["Wv"][:, 1024 * kc + 512 * nh:1024 * kc + 512 * (nh + 1)],
                                        start=(kc == 0), stop=False),
                                    wait=[tok_w["Wv"], war.get(("poolB", nh))] if kc == 0 else [],
                                    inc="T")
                for nh in range(2):
                    tv = p.emit("tensor",
                                lambda e, nh=nh: e.matmul(
                                    poolB[:, 512 * nh:512 * (nh + 1)], ones_for["bv"],
                                    brow["bv"][:, 512 * nh:512 * (nh + 1)],
                                    start=False, stop=True),
                                wait=(), inc="T")
                ev = p.emit("scalar", lambda e, buf=buf: e.copy(v16[buf][:], poolB[:]),
                            wait=[tv, war.get(("v16", 0))], inc="S")
                war[("poolB", 0)] = ev
                war[("poolB", 1)] = ev
                first = (tc == 0)
                last = (tc == NT - 1)
                if first:
                    tz_kv = p.emit("vector", lambda e: e.memset(kv_acc[:], 0.0),
                                   wait=[war.get("kv_acc")], inc="V")
                    war[("kv_acc", "z")] = tz_kv
                for pr in range(8):
                    tkv = p.emit("tensor",
                                 lambda e, pr=pr, buf=buf: e.matmul(
                                     poolK[:, 128 * pr:128 * (pr + 1)],
                                     k16[buf][:, 128 * pr:128 * (pr + 1)],
                                     v16[buf][:, 128 * pr:128 * (pr + 1)],
                                     start=True, stop=True),
                                 wait=([ap_, ev] + ([war.get("poolK")] if first else
                                                    [war.get(("poolK", "acc"))])), inc="T")
                war[("k16", 0)] = tkv
                war[("v16", 0)] = tkv
                tacc = p.emit("vector",
                              lambda e: e.tensor_tensor(kv_acc[:], kv_acc[:], poolK[:], ALU.add),
                              wait=[tkv, war.get(("kv_acc", "z"))], inc="V")
                war[("poolK", "acc")] = tacc
                war["kv_acc"] = tacc
                if last:
                    tok_kv_stopped = tacc
                pump(3)
            war["xnT"] = tkv
            pump(2)

        tz = p.emit("vector", lambda e: e.memset(kv16[:], 0.0),
                    wait=[war.get("kv16")], inc="V")
        evl = None
        for pr in range(8):
            e1 = p.emit("scalar",
                        lambda e, pr=pr: e.copy(kv16[0:64, 128 * pr:128 * pr + 64],
                                                kv_acc[0:64, 128 * pr:128 * pr + 64]),
                        wait=[tok_kv_stopped, tz], inc="S")
            evl = p.emit("scalar",
                         lambda e, pr=pr: e.copy(kv16[64:128, 128 * pr + 64:128 * (pr + 1)],
                                                 kv_acc[64:128, 128 * pr + 64:128 * (pr + 1)]),
                         wait=[e1], inc="S")
        war["poolK"] = war.get(("poolK", "acc"))
        war["kv_acc"] = evl
        return evl

    def phase_B(b, tok_kv, pump):
        if ddbg2 is not None and b == 0:
            p.emit("gpsimd", lambda e: e.dma_start(
                out=ddbg2[5][:, 0:1024], in_=kv16[:].rearrange("p (a f) -> p a f", a=1)[:, 0, :]),
                   wait=[tok_kv], inc="DG", n=16)
        for st4 in range(4):
            attn_ev = [None] * 8
            for mq in range(8):
                half = mq % 2
                psl = poolA[:, 512 * half:512 * (half + 1)]
                mm = p.emit("tensor",
                            lambda e, mq=mq, st4=st4, psl=psl: e.matmul(
                                psl, kv16[:, 128 * mq:128 * (mq + 1)],
                                q16[:, 2048 * mq + 512 * st4:2048 * mq + 512 * (st4 + 1)],
                                start=True, stop=True),
                            wait=[tok_kv, war.get(("poolA", half))], inc="T")
                ev = p.emit("scalar",
                            lambda e, mq=mq, psl=psl: e.copy(
                                attn16[:, 512 * mq:512 * (mq + 1)], psl),
                            wait=[mm, war.get("attn16")], inc="S")
                war[("poolA", half)] = ev
                attn_ev[mq] = ev

            last_tk = None
            for i in range(4):
                tc = 4 * st4 + i
                buf = tc % 2
                pp = poolB if buf == 0 else poolK
                ppk = "poolWo0" if buf == 0 else "poolWo1"
                extra = []
                if b == 0 and st4 == 0 and i < 2:
                    extra = [tok_kv, war.get(("poolB", 0)), war.get(("poolB", 1))]
                tk = None
                for kc in range(8):
                    for nh in range(2):
                        tk = p.emit("tensor",
                                    lambda e, kc=kc, nh=nh, i=i, pp=pp: e.matmul(
                                        pp[:, 512 * nh:512 * (nh + 1)],
                                        attn16[:, 512 * kc + 128 * i:512 * kc + 128 * (i + 1)],
                                        W16["Wo"][:, 1024 * kc + 512 * nh:1024 * kc + 512 * (nh + 1)],
                                        start=(kc == 0), stop=False),
                                    wait=[attn_ev[kc]] +
                                         (([tok_w["Wo"], war.get(ppk)] + extra)
                                          if kc == 0 else []) if nh == 0 else [],
                                    inc="T")
                for nh in range(2):
                    tk = p.emit("tensor",
                                lambda e, nh=nh, pp=pp: e.matmul(
                                    pp[:, 512 * nh:512 * (nh + 1)], ones_for["bo"],
                                    brow["bo"][:, 512 * nh:512 * (nh + 1)],
                                    start=False, stop=True),
                                wait=(), inc="T")
                last_tk = tk
                c1 = p.emit("scalar",
                            lambda e, buf=buf, pp=pp: e.activation(
                                ao_sb[buf][:], pp[:], AF.Copy, accum_out=TL["r1"][:]),
                            wait=[tk, war.get(("ao", 0)), war.get("lnstats")], inc="S")
                if ddbg2 is not None and b == 0 and tc == 0:
                    p.emit("gpsimd", lambda e: e.dma_start(out=ddbg2[4], in_=ao_sb[0][:]),
                           wait=[c1], inc="DG", n=16)
                c2 = p.emit("scalar",
                            lambda e, pp=pp: e.activation(junkS[:], pp[:], AF.Square,
                                                          accum_out=TL["r2"][:]),
                            wait=[tk, war.get("lnstats")], inc="S")
                war[ppk] = c2
                t_m = p.emit("vector", lambda e: e.tensor_scalar_mul(TL["r3"][:], TL["r1"][:], 1.0 / FD),
                             wait=[c1], inc="V")
                t_a = p.emit("vector", lambda e: e.tensor_scalar(TL["r4"][:], TL["r3"][:], TL["r3"][:],
                                                                 -1e-5, ALU.mult, ALU.add),
                             wait=[t_m], inc="V")
                t_b = p.emit("vector", lambda e: e.scalar_tensor_tensor(
                    TL["r4"][:], TL["r2"][:], 1.0 / FD, TL["r4"][:], ALU.mult, ALU.subtract),
                             wait=[t_a, c2], inc="V")
                t_l = p.emit("scalar", lambda e: e.activation(TL["r5"][:], TL["r4"][:], AF.Ln),
                             wait=[t_b], inc="S")
                t_i = p.emit("scalar", lambda e: e.activation(TL["r6"][:], TL["r5"][:], AF.Exp,
                                                              scale=-0.5),
                             wait=[t_l], inc="S")
                tok_xr = s_dma(xrs[buf][:], dx[b, 128 * tc:128 * (tc + 1), :],
                               wait=[war.get(("xrs", buf))], inc=f"XR{buf}")
                tt1 = p.emit("vector",
                             lambda e, buf=buf: e.tensor_scalar(
                                 t1b[:], ao_sb[buf][:], TL["r3"][:], TL["r6"][:],
                                 ALU.subtract, ALU.mult),
                             wait=[t_i, war.get("t1b")], inc="V")
                war[("ao", 0)] = tt1
                war["lnstats"] = tt1
                tx1 = p.emit("vector",
                             lambda e, buf=buf, tc=tc, b=b: e.scalar_tensor_tensor(
                                 f_sb[buf][:], t1b[:], 0.0, xrs[buf][:], ALU.add, ALU.add,
                                 accum_out=ST[b]["sum3"][:, tc:tc + 1]),
                             wait=[tt1, tok_xr, war.get(("f_sb", buf)),
                                   war.get(("f_sb", buf, "s")), war.get(("f_sb", buf, "d"))],
                             inc="V")
                war["t1b"] = tx1
                war[("xrs", buf)] = tx1
                td = s_dma(dx1[b, 128 * tc:128 * (tc + 1), :], f_sb[buf][:],
                           wait=[tx1], inc=f"DS{buf}")
                ts_ = p.emit("scalar",
                             lambda e, buf=buf, tc=tc, b=b: e.activation(
                                 ao_sb[buf][:], f_sb[buf][:], AF.Square,
                                 accum_out=ST[b]["sumsq3"][:, tc:tc + 1]),
                             wait=[tx1], inc="S")
                tr_ = p.emit("vector",
                             lambda e, buf=buf, tc=tc, b=b: e.scalar_tensor_tensor(
                                 ao_sb[buf][:], f_sb[buf][:], 0.0, rtb[:], ALU.add, ALU.mult,
                                 accum_out=ST[b]["xr"][:, tc:tc + 1]),
                             wait=[tx1, tok_rtb, ts_], inc="V")
                war[("ao", 0)] = tr_
                war[("f_sb", buf)] = tr_
                war[("f_sb", buf, "s")] = ts_
                war[("f_sb", buf, "d")] = td
                pump(3)
            war["attn16"] = last_tk
            pump(2)
        tok_ds_end = (("DS0", p.semv.get("DS0", 0)), ("DS1", p.semv.get("DS1", 0)))
        sb_ = ST[b]
        t_m3 = p.emit("vector", lambda e: e.tensor_scalar_mul(sb_["m3"][:], sb_["sum3"][:], 1.0 / FD),
                      inc="V")
        t_a = p.emit("vector", lambda e: e.tensor_tensor(RT["t_t"][:], sb_["m3"][:], sb_["m3"][:],
                                                         ALU.mult),
                     wait=[t_m3], inc="V")
        t_v = p.emit("vector", lambda e: e.scalar_tensor_tensor(
            RT["t_t"][:], sb_["sumsq3"][:], 1.0 / FD, RT["t_t"][:], ALU.mult, ALU.subtract),
                     wait=[t_a], inc="V")
        t_ve = p.emit("vector", lambda e: e.tensor_scalar_add(RT["e_t"][:], RT["t_t"][:], 1e-5),
                      wait=[t_v], inc="V")
        t_l = p.emit("scalar", lambda e: e.activation(RT["e_t"][:], RT["e_t"][:], AF.Ln),
                     wait=[t_ve], inc="S")
        t_i3 = p.emit("scalar", lambda e: e.activation(RT["e_t"][:], RT["e_t"][:], AF.Exp, scale=-0.5),
                      wait=[t_l], inc="S")
        t_mx = p.emit("vector", lambda e: e.tensor_scalar_max(RT["w_t"][:], RT["t_t"][:], 1e-30),
                      wait=[t_v], inc="V")
        t_l2 = p.emit("scalar", lambda e: e.activation(RT["w_t"][:], RT["w_t"][:], AF.Ln),
                      wait=[t_mx], inc="S")
        t_sq = p.emit("scalar", lambda e: e.activation(RT["w_t"][:], RT["w_t"][:], AF.Exp, scale=0.5),
                      wait=[t_l2], inc="S")
        t_nx = p.emit("vector", lambda e: e.scalar_tensor_tensor(
            RT["w_t"][:], RT["w_t"][:], 32.0, RT["e_t"][:], ALU.mult, ALU.mult),
                      wait=[t_sq, t_i3], inc="V")
        t_nc = p.emit("vector", lambda e: e.tensor_scalar_max(RT["w_t"][:], RT["w_t"][:], 1e-12),
                      wait=[t_nx], inc="V")
        t_rc = p.emit("vector", lambda e: e.reciprocal(RT["w_t"][:], RT["w_t"][:]),
                      wait=[t_nc], inc="V")
        t_al = p.emit("vector", lambda e: e.scalar_tensor_tensor(
            sb_["alpha"][:], RT["w_t"][:], 32.0, RT["e_t"][:], ALU.mult, ALU.mult),
                      wait=[t_rc], inc="V")
        t_s1 = p.emit("vector", lambda e: e.tensor_scalar(
            RT["t_t"][:], sb_["m3"][:], sumrt[:], None, ALU.mult),
                      wait=[t_m3, tok_srt], inc="V")
        t_s2 = p.emit("vector", lambda e: e.tensor_tensor(RT["t_t"][:], sb_["xr"][:], RT["t_t"][:],
                                                          ALU.subtract),
                      wait=[t_s1], inc="V")
        t_s3 = p.emit("vector", lambda e: e.tensor_tensor(sb_["s_t"][:], RT["t_t"][:], RT["e_t"][:],
                                                          ALU.mult),
                      wait=[t_s2, t_al], inc="V")
        return t_s3, tok_ds_end

    I32 = mybir.dt.int32

    SHUF = {sh: [min(i + sh, 31) for i in range(32)] for sh in (8, 4, 2, 1)}
    BC0 = [0] * 32

    def build_routing(b, tok_s, g):
        sb_ = ST[b]
        steps = []
        state = {}

        def qsum(red_tok, out_name):
            t1 = p.emit("vector", lambda e: e.transpose(g["qt2"][:], g["qt1"][:]),
                        wait=[red_tok], inc="V")
            t2 = p.emit("vector", lambda e: e.reduce_sum(g["tr2"], g["qt2"][:], AX.X),
                        wait=[t1], inc="V")
            t3 = p.emit("vector", lambda e, out_name=out_name: e.stream_shuffle(
                g[out_name], g["tr2"], BC0),
                        wait=[t2], inc="V")
            war[("rtr", b)] = t3
            return t3

        def qtree(red_tok, op, fill, out_name):
            t0 = p.emit("vector", lambda e: e.memset(g["tr1"], fill),
                        wait=[war.get(("rtr", b))], inc="V")
            t1 = p.emit("vector", lambda e: e.tensor_copy(g["tr1"][0:16, :], g["redc"][0:16, :]),
                        wait=[t0, red_tok], inc="V")
            for sh in (8, 4, 2, 1):
                t2 = p.emit("vector", lambda e, sh=sh: e.stream_shuffle(
                    g["tr2"], g["tr1"], SHUF[sh]),
                            wait=[t1], inc="V")
                t1 = p.emit("vector", lambda e: e.tensor_tensor(
                    g["tr1"][0:16, :], g["tr1"][0:16, :], g["tr2"][0:16, :], op),
                            wait=[t2], inc="V")
            t3 = p.emit("vector", lambda e, out_name=out_name: e.stream_shuffle(
                g[out_name], g["tr1"], BC0),
                        wait=[t1], inc="V")
            war[("rtr", b)] = t3
            return t3

        def init():
            tt = p.emit("tensor", lambda e: e.transpose(poolD[0:16, 0:128], sb_["s_t"][:],
                                                        idF32[:]),
                        wait=[tok_s, war.get("poolD"), tok_c], inc="T")
            tv = p.emit("vector", lambda e: e.tensor_copy(g["s16"][0:16, :], poolD[0:16, 0:128]),
                        wait=[tt], inc="V")
            war["poolD"] = tv
            t1 = p.emit("vector", lambda e: e.tensor_scalar_mul(g["sp16"][0:16, :],
                                                                g["s16"][0:16, :], 10.0),
                        wait=[tv], inc="V")
            t2 = p.emit("scalar", lambda e: e.activation(g["bp16"][0:16, :], g["sp16"][0:16, :],
                                                         AF.Relu),
                        wait=[t1], inc="S")
            t3 = p.emit("vector", lambda e: e.reduce_max(g["redc"][0:16, :], g["sp16"][0:16, :],
                                                         AX.X),
                        wait=[t1], inc="V")
            t4 = qtree(t3, ALU.max, -1e30, "nsmaxs")
            t5 = p.emit("vector", lambda e: e.tensor_scalar_mul(g["nsmaxs"], g["nsmaxs"],
                                                                -1.0),
                        wait=[t4], inc="V")
            t6 = p.emit("vector", lambda e: e.memset(g["a16"], 0.0), inc="V")
            t6b = p.emit("vector", lambda e: e.memset(g["qt1"][:], 0.0), inc="V")
            state["qz"] = t6b
            t7 = p.emit("vector", lambda e: e.memset(g["a10s"], 0.0), inc="V")
            state["t"] = (t2, t5, t6, t7)

        steps.append(init)

        def coor_iter(it):
            def fn():
                tprev = state["t"]
                t1 = p.emit("vector", lambda e: e.tensor_tensor(g["t16"][0:16, :], g["sp16"][0:16, :],
                                                                g["bp16"][0:16, :], ALU.subtract),
                            wait=list(tprev), inc="V")
                t2 = p.emit("vector", lambda e: e.tensor_tensor(g["nmxs"], g["a10s"],
                                                                g["nsmaxs"], ALU.max),
                            wait=list(tprev), inc="V")
                t3 = p.emit("scalar", lambda e: e.activation(g["e16"][0:16, :], g["t16"][0:16, :],
                                                             AF.Exp, bias=g["nmxs"][0:16, :],
                                                             accum_out=g["qt1"][0:16, 0:1]),
                            wait=[t1, t2, state.get("qz")], inc="S")
                t5 = qsum(t3, "trb")
                t6 = p.emit("scalar", lambda e: e.activation(g["lns"], g["trb"], AF.Ln),
                            wait=[t5], inc="S")
                t7 = p.emit("vector", lambda e: e.tensor_scalar(g["tmps"], g["lns"], -0.1,
                                                                CONST_R, ALU.mult, ALU.add),
                            wait=[t6], inc="V")
                t8 = p.emit("vector", lambda e: e.scalar_tensor_tensor(
                    g["a16"], g["nmxs"], 0.1, g["tmps"], ALU.mult, ALU.add),
                            wait=[t7], inc="V")
                t9 = p.emit("vector", lambda e: e.tensor_scalar_mul(g["a10s"], g["a16"], 10.0),
                            wait=[t8], inc="V")
                t10 = p.emit("scalar", lambda e: e.activation(g["bp16"][0:16, :], g["sp16"][0:16, :],
                                                              AF.Relu, bias=g["a10s"][0:16, :]),
                             wait=[t9], inc="S")
                state["t"] = (t10,)
            return fn

        for it in range(COOR_ITERS):
            steps.append(coor_iter(it))

        def keys():
            tprev = state["t"]
            t1 = p.emit("vector", lambda e: e.tensor_scalar(
                g["w16"][0:16, :], g["s16"][0:16, :], g["a16"][0:16, :], 0.0, ALU.add, ALU.min),
                        wait=list(tprev), inc="V")
            t2a = p.emit("vector", lambda e: e.tensor_scalar(
                g["km16"][0:16, :], g["w16"][0:16, :], 0.2, -CUT_W * 0.2, ALU.mult, ALU.add),
                         wait=[t1], inc="V")
            t2 = p.emit("scalar", lambda e: e.activation(g["km16"][0:16, :], g["km16"][0:16, :],
                                                         AF.Exp),
                        wait=[t2a], inc="S")
            t3 = p.emit("vector", lambda e: e.tensor_scalar(g["km16"][0:16, :], g["km16"][0:16, :],
                                                            0.5, -2e7, ALU.mult, ALU.add),
                        wait=[t2], inc="V")
            t4 = p.emit("vector", lambda e: e.tensor_scalar(g["gti16"][0:16, :], g["w16"][0:16, :],
                                                            float(CUT_W), None, ALU.is_gt),
                        wait=[t1], inc="V")
            t5a = p.emit("vector", lambda e: e.tensor_copy(g["key16"][0:16, :], g["km16"][0:16, :]),
                         wait=[t3], inc="V")
            t5 = p.emit("vector", lambda e: e.copy_predicated(g["key16"][0:16, :], g["gti16"][0:16, :],
                                                              g["w16"][0:16, :]),
                        wait=[t5a, t4], inc="V")
            t6 = p.emit("vector", lambda e: e.memset(g["los"], -2e7), inc="V")
            t7 = p.emit("vector", lambda e: e.memset(g["his"], 1.0), inc="V")
            state["t"] = (t5, t6, t7)

        steps.append(keys)

        def search_iter(it):
            def fn():
                tprev = state["t"]
                t1 = p.emit("vector", lambda e: e.tensor_scalar(
                    g["mids"], g["los"], g["his"], 0.5, ALU.add, ALU.mult),
                            wait=list(tprev), inc="V")
                t2 = p.emit("vector", lambda e: e.tensor_scalar(
                    g["e16"][0:16, :], g["key16"][0:16, :], g["mids"][0:16, :], None, ALU.is_ge),
                            wait=[t1], inc="V")
                t3 = p.emit("vector", lambda e: e.reduce_sum(g["qt1"][0:16, 0:1], g["e16"][0:16, :],
                                                             AX.X),
                            wait=[t2], inc="V")
                t4 = qsum(t3, "cnts")
                t5 = p.emit("vector", lambda e: e.tensor_scalar(
                    g["pr16"], g["cnts"], 1024.0, None, ALU.is_ge),
                            wait=[t4], inc="V")
                t5b = p.emit("vector", lambda e: e.tensor_scalar(
                    g["np16"], g["cnts"], 1024.0, None, ALU.is_lt),
                             wait=[t4], inc="V")
                t6 = p.emit("vector", lambda e: e.copy_predicated(g["los"], g["pr16"],
                                                                  g["mids"]),
                            wait=[t5], inc="V")
                t7 = p.emit("vector", lambda e: e.copy_predicated(g["his"], g["np16"],
                                                                  g["mids"]),
                            wait=[t5b, t6], inc="V")
                state["t"] = (t6, t7)
            return fn

        for it in range(SEARCH_ITERS):
            steps.append(search_iter(it))

        def tail():
            tprev = state["t"]
            t1 = p.emit("vector", lambda e: e.tensor_scalar(
                g["ii16"][0:16, :], g["key16"][0:16, :], g["los"][0:16, :], None, ALU.is_ge),
                        wait=list(tprev), inc="V")
            t2a = p.emit("vector", lambda e: e.memset(g["km16"][0:16, :], 1e30),
                         wait=[t1], inc="V")
            t2 = p.emit("vector", lambda e: e.copy_predicated(g["km16"][0:16, :], g["ii16"][0:16, :],
                                                              g["key16"][0:16, :]),
                        wait=[t2a], inc="V")
            t3 = p.emit("vector", lambda e: e.tensor_reduce(g["redc"][0:16, :], g["km16"][0:16, :],
                                                            AX.X, ALU.min),
                        wait=[t2], inc="V")
            t4 = qtree(t3, ALU.min, 1e30, "taus")
            t5 = p.emit("vector", lambda e: e.tensor_scalar(
                g["e16"][0:16, :], g["key16"][0:16, :], g["taus"][0:16, :], None, ALU.is_gt),
                        wait=[t4], inc="V")
            t6 = p.emit("vector", lambda e: e.reduce_sum(g["qt1"][0:16, 0:1], g["e16"][0:16, :],
                                                         AX.X),
                        wait=[t5], inc="V")
            t7 = qsum(t6, "mgts")
            t8 = p.emit("vector", lambda e: e.tensor_scalar(
                g["qts"], g["mgts"], -1.0, 1024.0, ALU.mult, ALU.add),
                        wait=[t7], inc="V")
            t9 = p.emit("vector", lambda e: e.tensor_copy(
                g["rowbuf"], g["taus"][0:1, 0:1].broadcast_to((1, 128))),
                        wait=[t4], inc="V")
            t10 = p.emit("tensor", lambda e: e.transpose(poolD[:, 48:49], g["rowbuf"],
                                                         idF32[0:1, 0:1]),
                         wait=[t9, war.get("poolD")], inc="T")
            t11 = p.emit("vector", lambda e: e.tensor_copy(CL["tau"][:], poolD[:, 48:49]),
                         wait=[t10], inc="V")
            t12 = p.emit("vector", lambda e: e.tensor_copy(
                g["rowbuf"], g["qts"][0:1, 0:1].broadcast_to((1, 128))),
                         wait=[t8, t10], inc="V")
            t13 = p.emit("tensor", lambda e: e.transpose(poolD[:, 49:50], g["rowbuf"],
                                                         idF32[0:1, 0:1]),
                         wait=[t12], inc="T")
            t14 = p.emit("vector", lambda e: e.tensor_copy(CL["quota"][:], poolD[:, 49:50]),
                         wait=[t13], inc="V")
            t15 = p.emit("tensor", lambda e: e.transpose(poolD[:, 32:48], g["key16"][0:16, :],
                                                         idF32[0:16, 0:16]),
                         wait=[t2, t13], inc="T")
            t16 = p.emit("vector", lambda e: e.tensor_copy(RT["key"][:], poolD[:, 32:48]),
                         wait=[t15], inc="V")
            t17 = p.emit("vector", lambda e: e.tensor_scalar(
                RT["gtm"][:], RT["key"][:], CL["tau"][:], None, ALU.is_gt),
                         wait=[t16, t11], inc="V")
            t18 = p.emit("vector", lambda e: e.tensor_scalar(
                RT["eq"][:], RT["key"][:], CL["tau"][:], None, ALU.is_equal),
                         wait=[t16, t11], inc="V")
            t19 = p.emit("vector", lambda e: e.tensor_copy(eq16[:], RT["eq"][:]),
                         wait=[t18], inc="V")
            t20 = p.emit("tensor", lambda e: e.matmul(poolD[:, 0:16], ones128, eq16[:],
                                                      start=True, stop=True),
                         wait=[t19, t16], inc="T")
            t21 = p.emit("vector", lambda e: e.tensor_copy(RT["colsum"][:], poolD[:, 0:16]),
                         wait=[t20], inc="V")
            t22 = p.emit("tensor", lambda e: e.matmul(poolD[:, 16:32], tri16, eq16[:],
                                                      start=True, stop=True),
                         wait=[t19, t21], inc="T")
            t23 = p.emit("vector", lambda e: e.memset(RT["prefx"][:, 0:1], 0.0),
                         wait=[t21], inc="V")
            for c in range(1, NT):
                t23 = p.emit("vector", lambda e, c=c: e.tensor_tensor(
                    RT["prefx"][:, c:c + 1], RT["prefx"][:, c - 1:c], RT["colsum"][:, c - 1:c],
                    ALU.add),
                             wait=[t23], inc="V")
            t24 = p.emit("vector", lambda e: e.tensor_tensor(RT["cum"][:], poolD[:, 16:32],
                                                             RT["prefx"][:], ALU.add),
                         wait=[t22, t23], inc="V")
            t25 = p.emit("vector", lambda e: e.tensor_scalar(
                RT["cum"][:], RT["cum"][:], CL["quota"][:], None, ALU.is_le),
                         wait=[t24, t14], inc="V")
            t26 = p.emit("vector", lambda e: e.tensor_tensor(RT["cum"][:], RT["cum"][:], RT["eq"][:],
                                                             ALU.mult),
                         wait=[t25], inc="V")
            t27 = p.emit("vector", lambda e: e.tensor_tensor(sb_["mask"][:], RT["gtm"][:],
                                                             RT["cum"][:], ALU.add),
                         wait=[t26], inc="V")
            war["poolD"] = t27
            state["mask"] = t27
            if ddbg is not None:
                tc1 = p.emit("vector", lambda e: e.tensor_copy(
                    RT["t_t"][:], CL["tau"][:].broadcast_to((128, NT))),
                             wait=[t27], inc="V")
                s_dma(ddbg[b, 0], sb_["s_t"][:], wait=[t27], inc="DS0")
                s_dma(ddbg[b, 1], RT["gtm"][:], wait=[t27], inc="DS0")
                s_dma(ddbg[b, 2], RT["key"][:], wait=[t27], inc="DS0")
                s_dma(ddbg[b, 3], sb_["mask"][:], wait=[t27], inc="DS0")
                s_dma(ddbg[b, 4], sb_["m3"][:], wait=[t27], inc="DS0")
                s_dma(ddbg[b, 5], sb_["alpha"][:], wait=[t27], inc="DS0")
                s_dma(ddbg[b, 6], RT["t_t"][:], wait=[tc1], inc="DS0")
                s_dma(ddbg[b, 7], sb_["xr"][:], wait=[t27], inc="DS0")

        steps.append(tail)
        return steps, state

    cstack = ExitStack()
    c2stack = ExitStack()
    wstack = ExitStack()

    def load_w1_stage(barrierA):
        w1l16 = cstack.enter_context(nc.sbuf_tensor("w1l16", [128, 8 * 512], F16))
        w1h16 = cstack.enter_context(nc.sbuf_tensor("w1h16", [128, 8 * 4096], F16))
        hh16 = cstack.enter_context(nc.sbuf_tensor("hh16", [128, 4 * 256], F16))
        hl16 = cstack.enter_context(nc.sbuf_tensor("hl16", [128, 4 * 256], F16))
        wtmp0 = wstack.enter_context(nc.sbuf_tensor("wtmp0", [128, 2048], F32))
        wtmp = [wtmp0, wtmp0]
        C = dict(w1l16=w1l16, w1h16=w1h16, hh16=hh16, hl16=hl16)
        wsteps = []
        state = {"w1l": None, "w1h": None}

        def w1l_step(q):
            def fn():
                buf = q % 2
                td = p.emit("gpsimd", lambda e: e.dma_start(
                    out=wtmp[buf][:].rearrange("p (k f) -> p k f", k=4),
                    in_=dW["w1l"].rearrange("(k p) f -> p k f", p=128)[:, 4 * q:4 * (q + 1), :]),
                    wait=[war.get(("wtmp", 0)), barrierA], inc="DW", n=16)
                t = None
                for j in range(4):
                    kc = 4 * q + j
                    t = p.emit("scalar",
                               lambda e, kc=kc, j=j: e.activation(
                                   w1l16[:, 512 * kc:512 * (kc + 1)], wtmp[buf][:, 512 * j:512 * (j + 1)],
                                   AF.Copy, scale=gl_c[:, kc:kc + 1]),
                               wait=[td], inc="S")
                war[("wtmp", 0)] = t
                state["w1l"] = t
            return fn

        def w1h_step(kc, hf):
            def fn():
                buf = (2 * kc + hf) % 2
                td = p.emit("gpsimd", lambda e: e.dma_start(
                    out=wtmp[buf][:], in_=dW["w1h"][128 * kc:128 * (kc + 1), 2048 * hf:2048 * (hf + 1)]),
                    wait=[war.get(("wtmp", 0)), barrierA], inc="DW", n=16)
                ts_ = p.emit("scalar",
                             lambda e: e.activation(
                                 w1h16[:, 4096 * kc + 2048 * hf:4096 * kc + 2048 * (hf + 1)],
                                 wtmp[buf][:], AF.Copy, scale=gh_c[:, kc:kc + 1]),
                             wait=[td], inc="S")
                war[("wtmp", 0)] = ts_
                state["w1h"] = ts_
            return fn

        for q in range(2):
            wsteps.append(w1l_step(q))
        for kc in range(8):
            for hf in range(2):
                wsteps.append(w1h_step(kc, hf))
        C["_wstate"] = state
        return C, wsteps

    def load_w2_stage(C, barrierB):
        w2l16 = c2stack.enter_context(nc.sbuf_tensor("w2l16", [128, 4 * 1024], F16, side="right"))
        w2h16 = c2stack.enter_context(nc.sbuf_tensor("w2h16", [128, 32 * 1024], F16, side="right"))
        zt16 = [c2stack.enter_context(nc.sbuf_tensor(f"zt16_{i}", [128, 8 * 256], F16, side="right"))
                for i in range(2)]
        z16_c = c2stack.enter_context(nc.sbuf_tensor("z16_c", [128, FD], F16, side="right"))
        z16_c2 = c2stack.enter_context(nc.sbuf_tensor("z16_c2", [128, FD], F16, side="right"))
        z16 = [z16_c, z16_c2]
        C.update(w2l16=w2l16, w2h16=w2h16, zt16=zt16, z16=z16)
        for kc in range(4):
            p.emit("gpsimd", lambda e, kc=kc: e.dma_start(
                out=w2h16[:, 1024 * kc:1024 * (kc + 1)],
                in_=dW["w2h"][128 * kc:128 * (kc + 1), :]),
                wait=[barrierB], inc="DH0", n=16)
        tok_h0 = ("DH0", p.semv["DH0"])
        for kc in range(4):
            p.emit("gpsimd", lambda e, kc=kc: e.dma_start(
                out=w2l16[:, 1024 * kc:1024 * (kc + 1)],
                in_=dW["w2l"][128 * kc:128 * (kc + 1), :]),
                wait=[barrierB], inc="DL", n=16)
        tok_l = ("DL", p.semv["DL"])
        for kc in range(4, 16):
            p.emit("gpsimd", lambda e, kc=kc: e.dma_start(
                out=w2h16[:, 1024 * kc:1024 * (kc + 1)],
                in_=dW["w2h"][128 * kc:128 * (kc + 1), :]),
                wait=[barrierB], inc="DH1", n=16)
        tok_h1 = ("DH1", p.semv["DH1"])
        for kc in range(16, 32):
            p.emit("gpsimd", lambda e, kc=kc: e.dma_start(
                out=w2h16[:, 1024 * kc:1024 * (kc + 1)],
                in_=dW["w2h"][128 * kc:128 * (kc + 1), :]),
                wait=[barrierB], inc="DH2", n=16)
        tok_h2 = ("DH2", p.semv["DH2"])
        C["tok_w2l"] = tok_l
        C["tok_w2h"] = [tok_h0 if kc < 4 else tok_h1 if kc < 16 else tok_h2
                        for kc in range(32)]
        C["tok_w1l"] = C["_wstate"]["w1l"]
        C["tok_w1h"] = C["_wstate"]["w1h"]
        return C

    def phase_C(b, C, mask_getter, tok_ds_end, pump, first_c=False):
        sb_ = ST[b]
        pool16r = pool16[:].rearrange("p (k t) -> p k t", k=8)

        def zload(tc):
            buf = tc % 2
            tok = s_dma(xin[buf][:], dx1[b, 128 * tc:128 * (tc + 1), :],
                        wait=[war.get(("xin", buf))] +
                             (list(tok_ds_end) if tc < 2 else []), inc=f"DX{buf}")
            war[("xz", tc)] = tok

        def zprep(st2):
            par = st2 % 2
            zt = C["zt16"][par]
            zt3 = zt[:].rearrange("p (k t) -> p k t", k=8)
            ev_last = None
            for i in range(2):
                tc = 2 * st2 + i
                buf = tc % 2
                tz = p.emit("vector",
                            lambda e, buf=buf, tc=tc: e.tensor_scalar(
                                C["z16"][buf][:], xin[buf][:], sb_["m3"][:, tc:tc + 1],
                                sb_["alpha"][:, tc:tc + 1], ALU.subtract, ALU.mult),
                            wait=[war.get(("xz", tc)), war.get(("z16", buf))], inc="V")
                war[("xin", buf)] = tz
                tt = None
                for kc in range(8):
                    tt = p.emit("tensor",
                                lambda e, buf=buf, kc=kc: e.transpose(
                                    pool16[:, 128 * kc:128 * (kc + 1)],
                                    C["z16"][buf][:, 128 * kc:128 * (kc + 1)], id16),
                                wait=[tz, war.get("pool16")] +
                                     ([war.get("poolC_first")] if (first_c and st2 == 0) else []),
                                inc="T")
                war[("z16", buf)] = tt
                ev = p.emit("scalar",
                            lambda e, i=i, zt3=zt3: e.copy(zt3[:, :, 128 * i:128 * (i + 1)],
                                                           pool16r),
                            wait=[tt, war.get(("zt16", par))], inc="S")
                war["pool16"] = ev
                ev_last = ev
            war[("ztr", par)] = ev_last

        zload(0)
        zload(1)
        zprep(0)
        for st2 in range(8):
            par = st2 % 2
            tok_zt = war.get(("ztr", par))
            ztc = C["zt16"][par]
            if 2 * st2 + 2 < NT:
                zload(2 * st2 + 2)
                zload(2 * st2 + 3)
            for hh in range(8):
                ge_toks = [None] * 4
                for mh in range(4):
                    hcol = 4 * hh + mh
                    half = mh % 2
                    psl = poolK[:, 0:256] if half == 0 else poolK[:, 512:768]
                    tk = None
                    for kc in range(8):
                        tk = p.emit("tensor",
                                    lambda e, hcol=hcol, kc=kc, psl=psl, ztc=ztc: e.matmul(
                                        psl,
                                        C["w1h16"][:, 4096 * kc + 128 * hcol:4096 * kc + 128 * (hcol + 1)],
                                        ztc[:, 256 * kc:256 * (kc + 1)],
                                        start=(kc == 0), stop=(kc == 7)),
                                    wait=[tok_zt, war.get(("poolK", half)), C["tok_w1h"]]
                                    if kc == 0 else [], inc="T")
                    ge = p.emit("scalar",
                                lambda e, hcol=hcol, mh=mh, psl=psl: e.activation(
                                    C["hh16"][:, 256 * mh:256 * (mh + 1)],
                                    psl, AF.Gelu, bias=b1h_c[:, hcol:hcol + 1]),
                                wait=[tk, war.get(("hh16", mh))], inc="S")
                    war[("poolK", half)] = ge
                    ge_toks[mh] = ge
                for i in range(2):
                    tc = 2 * st2 + i
                    pp = poolA if i == 0 else poolB
                    ppn = "poolA" if i == 0 else "poolB"
                    tk = None
                    for kc in range(4):
                        hcol = 4 * hh + kc
                        for nh in range(2):
                            w_ = [ge_toks[kc], C["tok_w2h"][hcol] if (first_c and st2 == 0) else None]
                            if hh == 0 and kc == 0 and nh == 0:
                                w_ += [war.get((ppn, "c"))]
                                if first_c and st2 == 0:
                                    w_ += [war.get(("poolC", ppn))]
                            tk = p.emit("tensor",
                                        lambda e, kc=kc, hcol=hcol, nh=nh, i=i, pp=pp, hh=hh: e.matmul(
                                            pp[:, 512 * nh:512 * (nh + 1)],
                                            C["hh16"][:, 256 * kc + 128 * i:256 * kc + 128 * (i + 1)],
                                            C["w2h16"][:, 1024 * hcol + 512 * nh:1024 * hcol + 512 * (nh + 1)],
                                            start=(hh == 0 and kc == 0), stop=False),
                                        wait=w_, inc="T")
                    if hh == 7:
                        for nh in range(2):
                            tk = p.emit("tensor",
                                        lambda e, nh=nh, pp=pp: e.matmul(
                                            pp[:, 512 * nh:512 * (nh + 1)], ones_for["b2h"],
                                            brow["b2h"][:, 512 * nh:512 * (nh + 1)],
                                            start=False, stop=True),
                                        wait=(), inc="T")
                        war[(ppn, "h")] = tk
                    for kc in range(4):
                        war[("hh16", kc)] = tk
                pump(2)
                if hh == 3 and st2 < 7:
                    zprep(st2 + 1)
            pump(6)
            tok_hv = [None, None]
            for i in range(2):
                tc = 2 * st2 + i
                pp = poolA if i == 0 else poolB
                ppn = "poolA" if i == 0 else "poolB"
                tok_hv[i] = p.emit("vector",
                                   lambda e, i=i, tc=tc, pp=pp: e.tensor_scalar(
                                       hvm[i][:], pp[:], sb_["mask"][:, tc:tc + 1], None,
                                       ALU.mult),
                                   wait=[war.get((ppn, "h")), mask_getter(), war.get(("hvm", i))],
                                   inc="V")
            if ddbg2 is not None and b == 0 and st2 == 0:
                p.emit("gpsimd", lambda e: e.dma_start(out=ddbg2[3], in_=C["zt16"][0][:, 0:1024]),
                       wait=[tok_zt], inc="DG", n=16)
            lge = [None] * 4
            for ml in range(4):
                half = ml % 2
                psl = poolK[:, 0:256] if half == 0 else poolK[:, 512:768]
                tk = None
                for kc in range(8):
                    tk = p.emit("tensor",
                                lambda e, ml=ml, kc=kc, psl=psl, ztc=ztc: e.matmul(
                                    psl,
                                    C["w1l16"][:, 512 * kc + 128 * ml:512 * kc + 128 * (ml + 1)],
                                    ztc[:, 256 * kc:256 * (kc + 1)],
                                    start=(kc == 0), stop=(kc == 7)),
                                wait=[war.get(("poolK", half)), C["tok_w1l"],
                                      C["tok_w2l"]] if kc == 0 else [],
                                inc="T")
                ge = p.emit("scalar",
                            lambda e, ml=ml, psl=psl: e.activation(
                                C["hl16"][:, 256 * ml:256 * (ml + 1)], psl, AF.Gelu,
                                bias=b1l_c[:, ml:ml + 1]),
                            wait=[tk, war.get(("hl16", ml))], inc="S")
                war[("poolK", half)] = ge
                lge[ml] = ge
            war[("zt16", par)] = tk
            for i in range(2):
                tc = 2 * st2 + i
                buf = tc % 2
                pp = poolA if i == 0 else poolB
                ppn = "poolA" if i == 0 else "poolB"
                tk = None
                for kc in range(4):
                    for nh in range(2):
                        w_ = [lge[kc]]
                        if kc == 0 and nh == 0:
                            w_ += [tok_hv[i]]
                        tk = p.emit("tensor",
                                    lambda e, kc=kc, nh=nh, i=i, pp=pp: e.matmul(
                                        pp[:, 512 * nh:512 * (nh + 1)],
                                        C["hl16"][:, 256 * kc + 128 * i:256 * kc + 128 * (i + 1)],
                                        C["w2l16"][:, 1024 * kc + 512 * nh:1024 * kc + 512 * (nh + 1)],
                                        start=(kc == 0), stop=False),
                                    wait=w_, inc="T")
                for nh in range(2):
                    tk = p.emit("tensor",
                                lambda e, nh=nh, pp=pp: e.matmul(
                                    pp[:, 512 * nh:512 * (nh + 1)], ones_for["b2l"],
                                    brow["b2l"][:, 512 * nh:512 * (nh + 1)],
                                    start=False, stop=True),
                                wait=(), inc="T")
                for kc in range(4):
                    war[("hl16", kc)] = tk
                if ddbg2 is not None and b == 0 and st2 == 0 and i == 0:
                    p.emit("gpsimd", lambda e: e.dma_start(out=ddbg2[0], in_=C["z16"][0][:]),
                           wait=[war.get(("z16", 0)), tok_hv[0]], inc="DG", n=16)
                    p.emit("gpsimd", lambda e: e.dma_start(out=ddbg2[1], in_=hvm[0][:]),
                           wait=[tok_hv[0], tk], inc="DG", n=16)
                tf = p.emit("vector",
                            lambda e, i=i, tc=tc, pp=pp: e.scalar_tensor_tensor(
                                f_sb[i][:], pp[:], 0.0, hvm[i][:], ALU.add, ALU.add,
                                accum_out=sb_["sum4"][:, tc:tc + 1]),
                            wait=[tk, tok_hv[i], war.get(("f_sb", i)),
                                  war.get(("f_sb", i, "s")), war.get(("f_sb", i, "d")),
                                  war.get(("f_sb", i, "o"))], inc="V")
                war[(ppn, "c")] = tf
                war[("hvm", i)] = tf
                if ddbg2 is not None and b == 0 and st2 == 0 and i == 0:
                    p.emit("gpsimd", lambda e: e.dma_start(out=ddbg2[2], in_=f_sb[0][:]),
                           wait=[tf], inc="DG", n=16)
                ts_ = p.emit("scalar",
                             lambda e, i=i, tc=tc: e.activation(
                                 junkS[:], f_sb[i][:], AF.Square,
                                 accum_out=sb_["sumsq4"][:, tc:tc + 1]),
                             wait=[tf], inc="S")
                t_m = p.emit("vector", lambda e, tc=tc: e.tensor_scalar_mul(
                    T4["m4"][:], sb_["sum4"][:, tc:tc + 1], 1.0 / FD),
                             wait=[tf, war.get("t4")], inc="V")
                t_a = p.emit("vector", lambda e: e.tensor_scalar(T4["v4"][:], T4["m4"][:],
                                                                 T4["m4"][:], -1e-5, ALU.mult, ALU.add),
                             wait=[t_m], inc="V")
                t_b = p.emit("vector", lambda e, tc=tc: e.scalar_tensor_tensor(
                    T4["v4"][:], sb_["sumsq4"][:, tc:tc + 1], 1.0 / FD, T4["v4"][:],
                    ALU.mult, ALU.subtract),
                             wait=[t_a, ts_], inc="V")
                t_l = p.emit("scalar", lambda e: e.activation(T4["w4"][:], T4["v4"][:], AF.Ln),
                             wait=[t_b], inc="S")
                t_i = p.emit("scalar", lambda e: e.activation(T4["i4"][:], T4["w4"][:], AF.Exp,
                                                              scale=-0.5),
                             wait=[t_l], inc="S")
                tok_xr = s_dma(xrs[i][:], dx1[b, 128 * tc:128 * (tc + 1), :],
                               wait=[war.get(("xrs", i))] +
                                    (list(tok_ds_end) if st2 == 0 else []), inc=f"XR{i}")
                tt1 = p.emit("vector",
                             lambda e, i=i: e.tensor_scalar(
                                 hvm[i][:], f_sb[i][:], T4["m4"][:], T4["i4"][:],
                                 ALU.subtract, ALU.mult),
                             wait=[t_i, war.get(("hvm", i))], inc="V")
                war[("f_sb", i)] = tt1
                war["t4"] = tt1
                to_ = p.emit("vector",
                             lambda e, i=i: e.tensor_tensor(f_sb[i][:], hvm[i][:], xrs[i][:], ALU.add),
                             wait=[tt1, tok_xr, ts_, war.get(("f_sb", i, "o")),
                                   war.get(("f_sb", i, "d"))], inc="V")
                war[("hvm", i)] = to_
                war[("xrs", i)] = to_
                td = s_dma(dout[b, 128 * tc:128 * (tc + 1), :], f_sb[i][:], wait=[to_],
                           inc=f"DO{i}")
                war[("f_sb", i, "o")] = to_
                war[("f_sb", i, "d")] = td

    routing_q = []

    def pump(k):
        for _ in range(min(k, len(routing_q))):
            routing_q.pop(0)()

    def pump_all():
        while routing_q:
            routing_q.pop(0)()

    def nopump(k):
        return None

    tok_kv0 = phase_A(0, nopump)
    tok_s0, tok_ds0 = phase_B(0, tok_kv0, nopump)
    steps0, rstate0 = build_routing(0, tok_s0, RB0)
    routing_q.extend(steps0)
    tok_kv1 = phase_A(1, pump)
    barrierA = [("T", p.semv.get("T", 0)), ("V", p.semv.get("V", 0)),
                ("S", p.semv.get("S", 0))]
    ab1.close()
    C, wsteps = load_w1_stage(barrierA[0])
    p.wait("gpsimd", *barrierA[1:])
    merged = []
    ri = wi = 0
    while wi < len(wsteps) or ri < len(routing_q):
        if wi < len(wsteps):
            merged.append(wsteps[wi]); wi += 1
        if ri < len(routing_q):
            merged.append(routing_q[ri]); ri += 1
        if wi < len(wsteps):
            merged.append(wsteps[wi]); wi += 1
    routing_q[:] = merged
    tok_s1, tok_ds1 = phase_B(1, tok_kv1, pump)

    barrierB = [("T", p.semv.get("T", 0)), ("V", p.semv.get("V", 0)),
                ("S", p.semv.get("S", 0))]
    war[("poolC", "poolA")] = barrierB[1]
    war[("poolC", "poolB")] = barrierB[2]
    war["poolC_first"] = barrierB[2]
    ab2.close()
    wstack.close()
    p.wait("gpsimd", *barrierB)
    C = load_w2_stage(C, barrierB[0])

    steps1, rstate1 = build_routing(1, tok_s1, RB0)
    routing_q.extend(steps1)
    phase_C(0, C, lambda: rstate0["mask"], tok_ds0, pump, first_c=True)
    pump_all()
    phase_C(1, C, lambda: rstate1["mask"], tok_ds1, nopump)

    for s in ("DO0", "DO1", "DS0", "DS1"):
        if s in p.semv:
            p.wait("sync", (s, p.semv[s]))

    with nc.Block() as block:
        @block.sync
        def _(e):
            for f in p.seq["sync"]:
                f(e)

        @block.gpsimd
        def _(e):
            for f in p.seq["gpsimd"]:
                f(e)

        @block.tensor
        def _(e):
            for f in p.seq["tensor"]:
                f(e)

        @block.vector
        def _(e):
            for f in p.seq["vector"]:
                f(e)

        @block.scalar
        def _(e):
            for f in p.seq["scalar"]:
                f(e)

    cstack.close()
    c2stack.close()
    return nc, stack


def make_consts():
    c = np.zeros((128, 512), np.float32)
    c[:, 0:128] = np.eye(128, dtype=np.float32)
    c[:, 128:256] = np.triu(np.ones((128, 128), np.float32))
    c[:, 256:384] = 1.0
    for m in range(8):
        for ptn in range(128):
            h = 2 * m + ptn // 64
            c[ptn, 384 + 16 * m + h] = 1.0
    hmT = np.zeros((16, 8 * 128), np.float32)
    for m in range(8):
        for ptn in range(128):
            hmT[2 * m + ptn // 64, 128 * m + ptn] = 1.0
    return c, hmT


_PROG_CACHE = {}


def kernel(L0_attn_x, L0_attn_Wq, L0_attn_bq, L0_attn_Wk, L0_attn_bk, L0_attn_Wv,
           L0_attn_bv, L0_attn_Wo, L0_attn_bo, L0_attn_m, L0_ffn_routing_token,
           L0_ffn_gamma_light, L0_ffn_w1_light, L0_ffn_b1_light, L0_ffn_w2_light,
           L0_ffn_b2_light, L0_ffn_gamma_heavy, L0_ffn_w1_heavy, L0_ffn_b1_heavy,
           L0_ffn_w2_heavy, L0_ffn_b2_heavy, _debug=False):
    key = ("prog", _debug)
    if key not in _PROG_CACHE:
        _PROG_CACHE[key] = build_program(debug=_debug)
    nc, _stack = _PROG_CACHE[key]
    cc, hmT = make_consts()
    x = np.ascontiguousarray(np.asarray(L0_attn_x, dtype=np.float32))
    rtbv = np.ascontiguousarray(np.broadcast_to(
        np.asarray(L0_ffn_routing_token, np.float32)[None, :], (128, FD)))
    base = {
        "Wq": np.ascontiguousarray(L0_attn_Wq), "Wk": np.ascontiguousarray(L0_attn_Wk),
        "Wv": np.ascontiguousarray(L0_attn_Wv), "Wo": np.ascontiguousarray(L0_attn_Wo),
        "w1l": np.ascontiguousarray(L0_ffn_w1_light), "w2l": np.ascontiguousarray(L0_ffn_w2_light),
        "w1h": np.ascontiguousarray(L0_ffn_w1_heavy), "w2h": np.ascontiguousarray(L0_ffn_w2_heavy),
        "bq": np.ascontiguousarray(L0_attn_bq), "bk": np.ascontiguousarray(L0_attn_bk),
        "bv": np.ascontiguousarray(L0_attn_bv), "bo": np.ascontiguousarray(L0_attn_bo),
        "b1l": np.ascontiguousarray(L0_ffn_b1_light), "b2l": np.ascontiguousarray(L0_ffn_b2_light),
        "b1h": np.ascontiguousarray(L0_ffn_b1_heavy), "b2h": np.ascontiguousarray(L0_ffn_b2_heavy),
        "gl": np.ascontiguousarray(L0_ffn_gamma_light), "gh": np.ascontiguousarray(L0_ffn_gamma_heavy),
        "mh": np.ascontiguousarray(L0_attn_m), "rtb": rtbv, "c16": cc, "hmT": hmT,
    }
    in_maps = []
    for c in range(8):
        d = dict(base)
        d["x"] = x[NB * c:NB * (c + 1)]
        in_maps.append(d)
    res = run_bass_kernel_spmd(nc, in_maps, core_ids=list(range(8)))
    out = np.concatenate([np.asarray(r["out"]) for r in res.results], axis=0)
    if _debug:
        dbg = np.stack([np.asarray(r["dbg"]) for r in res.results], axis=0)
        dbg2 = np.stack([np.asarray(r["dbg2"]) for r in res.results], axis=0)
        np.save('/tmp/dev_dbg2.npy', dbg2)
        x1s = np.stack([np.asarray(r["x1s"]) for r in res.results], axis=0)
        np.save('/tmp/dev_x1s.npy', x1s)
        return out.astype(np.float32, copy=False), dbg
    return out.astype(np.float32, copy=False)


# version 21 (final)
# speedup vs baseline: 1.0296x; 1.0296x over previous
import numpy as np
from contextlib import ExitStack

import concourse.bass as bass
import concourse.mybir as mybir
from concourse.bass_utils import run_bass_kernel_spmd

F32 = mybir.dt.float32
F16 = mybir.dt.float16
AF = mybir.ActivationFunctionType
ALU = mybir.AluOpType
AX = mybir.AxisListType

NB = 2
N = 2048
FD = 1024
H = 16
LH = 512
HHID = 4096
NT = N // 128
CUT_W = float(np.float32(-8.733654475))
LN2048 = float(np.log(np.float64(2048.0)))
CONST_R = float(np.float32(0.1) * np.log(np.float32(8.0)))
COOR_ITERS = 50
SEARCH_ITERS = 46

ENGS = ("sync", "gpsimd", "tensor", "vector", "scalar")


class Plan:
    def __init__(self, nc, stack):
        self.nc = nc
        self.stack = stack
        self.seq = {k: [] for k in ENGS}
        self.sems = {}
        self.semv = {}
        self.waited = {k: {} for k in ENGS}

    def sem(self, name):
        if name not in self.sems:
            self.sems[name] = self.stack.enter_context(self.nc.semaphore(name))
            self.semv[name] = 0
        return self.sems[name]

    DMA_SEMS = ("DG", "DX0", "DX1", "XR0", "XR1", "DS0", "DS1",
                "DO0", "DO1", "DW", "DH1", "DH2",
                "DC", "DB", "WQ", "WK", "WV", "WO", "DL", "DH0")

    def emit(self, eng, fn, wait=(), inc=None, n=1):
        if inc is not None:
            self.sem(inc)
        is_dma = inc in self.DMA_SEMS
        w = []
        for tok in wait:
            if tok is None:
                continue
            s, v = tok
            self.sem(s)
            if self.waited[eng].get(s, 0) < v or (is_dma and not w):
                self.waited[eng][s] = max(self.waited[eng].get(s, 0), v)
                w.append((s, v))
        if is_dma and not w:
            self.sem("Z")
            w.append(("Z", 1))
        tok = None
        if inc is not None:
            self.semv[inc] += n
            tok = (inc, self.semv[inc])
        sems = self.sems

        def run(e, w=tuple(w), fn=fn, inc=inc, n=n):
            for s, v in w:
                e.wait_ge(sems[s], v)
            instr = fn(e)
            if inc is not None:
                instr.then_inc(sems[inc], n)

        self.seq[eng].append(run)
        return tok

    def wait(self, eng, *toks):
        w = []
        for tok in toks:
            if tok is None:
                continue
            s, v = tok
            self.sem(s)
            if self.waited[eng].get(s, 0) < v:
                self.waited[eng][s] = v
                w.append((s, v))
        if not w:
            return
        sems = self.sems

        def run(e, w=tuple(w)):
            for s, v in w:
                e.wait_ge(sems[s], v)

        self.seq[eng].append(run)


def build_program(debug=False):
    stack = ExitStack()
    nc = bass.Bass("TRN2", target_bir_lowering=False,
                   detect_race_conditions=False)
    p = Plan(nc, stack)

    dx = nc.dram_tensor("x", [NB, N, FD], F32, kind="ExternalInput").ap()
    dW = {nm: nc.dram_tensor(nm, shp, F32, kind="ExternalInput").ap()
          for nm, shp in [("Wq", [FD, FD]), ("Wk", [FD, FD]), ("Wv", [FD, FD]),
                          ("Wo", [FD, FD]), ("w1l", [FD, LH]), ("w2l", [LH, FD]),
                          ("w1h", [FD, HHID]), ("w2h", [HHID, FD])]}
    dB = {nm: nc.dram_tensor(nm, [ln_], F32, kind="ExternalInput").ap()
          for nm, ln_ in [("bq", FD), ("bk", FD), ("bv", FD), ("bo", FD),
                          ("b1l", LH), ("b2l", FD), ("b1h", HHID), ("b2h", FD),
                          ("gl", FD), ("gh", FD)]}
    dm = nc.dram_tensor("mh", [H], F32, kind="ExternalInput").ap()
    drt = nc.dram_tensor("rtb", [128, FD], F32, kind="ExternalInput").ap()
    dc16 = nc.dram_tensor("c16", [128, 512], F32, kind="ExternalInput").ap()
    dhmT = nc.dram_tensor("hmT", [16, 8 * 128], F32, kind="ExternalInput").ap()
    dx1 = nc.dram_tensor("x1s", [NB, N, FD], F32,
                         kind="ExternalOutput" if debug else "Internal").ap()
    dout = nc.dram_tensor("out", [NB, N, FD], F32, kind="ExternalOutput").ap()
    ddbg = None
    ddbg2 = None
    if debug:
        ddbg = nc.dram_tensor("dbg", [NB, 8, 128, NT], F32, kind="ExternalOutput").ap()
        ddbg2 = nc.dram_tensor("dbg2", [6, 128, FD], F32, kind="ExternalOutput").ap()

    def sbuf(name, shape, dt):
        return stack.enter_context(nc.sbuf_tensor(name, shape, dt))

    c16 = sbuf("c16s", [128, 512], F16)
    idF32 = sbuf("idF32", [128, 128], F32)
    id16 = c16[:, 0:128]
    tri16 = c16[:, 128:256]
    ones128 = c16[:, 256:384]
    ones_row = c16[0:1, 256:384]
    ones_rows = {0: c16[0:1, 256:384], 32: c16[32:33, 256:384], 64: c16[64:65, 256:384]}
    bq_c = sbuf("bq_c", [128, 8], F32)
    b1l_c = sbuf("b1l_c", [128, 4], F32)
    b1h_c = sbuf("b1h_c", [128, 32], F32)
    gl_c = sbuf("gl_c", [128, 8], F32)
    gh_c = sbuf("gh_c", [128, 8], F32)
    brow_a = sbuf("brow_a", [65, FD], F16)
    brow_b = sbuf("brow_b", [65, FD], F16)
    brow = {"bk": brow_a[0:1, :], "bv": brow_a[32:33, :], "bo": brow_a[64:65, :],
            "b2l": brow_b[0:1, :], "b2h": brow_b[32:33, :]}
    ones_for = {"bk": ones_rows[0], "bv": ones_rows[32], "bo": ones_rows[64],
                "b2l": ones_rows[0], "b2h": ones_rows[32]}
    m16 = sbuf("m16", [16, 1], F32)
    rscale = sbuf("rscale", [16, 1], F32)
    sumrt = sbuf("sumrt", [128, 1], F32)
    junkS = sbuf("junkS", [128, FD], F16)

    ST = [{nm: sbuf(f"st{b}_" + nm, [128, NT], F32)
           for nm in ("sum3", "sumsq3", "xr", "m3", "alpha", "s_t", "sp", "bp",
                      "mask", "sum4", "sumsq4")} for b in range(NB)]
    RT = {nm: sbuf("rt_" + nm, [128, NT], F32)
          for nm in ("t_t", "e_t", "w_t", "key", "eq", "gtm", "cum",
                     "colsum", "prefx")}
    eq16 = sbuf("rt_eq16", [128, NT], F16)
    I32 = mybir.dt.int32
    CL = {nm: sbuf("cl_" + nm, [128, 1], F32)
          for nm in ("t1", "t2", "tau", "quota")}
    TL = {nm: sbuf("tl_" + nm, [128, 1], F32) for nm in ("r1", "r2", "r3", "r4", "r5", "r6")}
    T4 = {nm: sbuf("t4_" + nm, [128, 1], F32) for nm in ("m4", "i4", "v4", "w4")}

    xin = [sbuf(f"xin{i}", [128, FD], F32) for i in range(2)]
    xrs = [sbuf(f"xrs{i}", [128, FD], F32) for i in range(2)]
    f_sb = [sbuf(f"f_sb{i}", [128, FD], F32) for i in range(2)]
    hvm = [sbuf(f"hvm{i}", [128, FD], F32) for i in range(2)]

    pool16 = stack.enter_context(nc.psum_tensor("pool16", [128, 1024], F16))
    poolA = stack.enter_context(nc.psum_tensor("poolA", [128, 1024], F32))
    poolB = stack.enter_context(nc.psum_tensor("poolB", [128, 1024], F32))
    poolK = stack.enter_context(nc.psum_tensor("poolK", [128, 1024], F32))
    poolD = stack.enter_context(nc.psum_tensor("poolD", [128, 512], F32))

    def mk_routing_bufs(scope, tag, side=None):
        g = {}
        for nm in ("s16", "sp16", "bp16", "t16", "e16", "w16", "km16"):
            g[nm] = scope.enter_context(nc.sbuf_tensor(f"rb{tag}_{nm}", [32, 128], F32, side=side))
        g["key16"] = scope.enter_context(nc.sbuf_tensor(f"rb{tag}_key16", [32, 128], F32, side=side))
        g["gti16"] = scope.enter_context(nc.sbuf_tensor(f"rb{tag}_gti16", [32, 128], I32, side=side))
        g["ii16"] = g["gti16"]
        scal = scope.enter_context(nc.sbuf_tensor(f"rb{tag}_scal", [32, 24], F32, side=side))
        names = ("a16", "a10s", "nmxs", "nsmaxs", "los", "his", "mids", "taus",
                 "cnts", "mgts", "qts", "redc", "tr1", "tr2", "trb", "lns", "tmps")
        for j, nm in enumerate(names):
            g[nm] = scal[:, j:j + 1]
        iscal = scope.enter_context(nc.sbuf_tensor(f"rb{tag}_iscal", [32, 2], I32, side=side))
        g["pr16"] = iscal[:, 0:1]
        g["np16"] = iscal[:, 1:2]
        g["rowbuf"] = g["t16"][0:1, :]
        g["qt1"] = scope.enter_context(nc.sbuf_tensor(f"rb{tag}_qt1", [32, 32], F32, side=side))
        g["qt2"] = scope.enter_context(nc.sbuf_tensor(f"rb{tag}_qt2", [32, 32], F32, side=side))
        return g


    RB0 = mk_routing_bufs(stack, "0", side="right")

    ab1 = ExitStack()
    ab2 = ExitStack()
    hmT16 = ab1.enter_context(nc.sbuf_tensor("hmT16", [16, 8 * 128], F16))
    qinv = ab1.enter_context(nc.sbuf_tensor("qinv", [16, 512], F32))
    qinv16 = ab1.enter_context(nc.sbuf_tensor("qinv16", [16, 512], F16))
    kstat = ab1.enter_context(nc.sbuf_tensor("kstat", [128, 16], F32))
    W16 = {}
    for nm in ("Wq", "Wk", "Wv"):
        W16[nm] = ab1.enter_context(nc.sbuf_tensor(nm + "_16", [128, 8 * 1024], F16))
    W16["Wo"] = ab2.enter_context(nc.sbuf_tensor("Wo_16", [128, 8 * 1024], F16, side="right"))
    xnT = ab1.enter_context(nc.sbuf_tensor("xnT", [128, 8 * 512], F16))
    xn16_0 = ab1.enter_context(nc.sbuf_tensor("xn16_0", [128, FD], F16))
    xn16_1 = ab1.enter_context(nc.sbuf_tensor("xn16_1", [128, FD], F16))
    xn16 = [xn16_0, xn16_1]
    kv_acc = ab1.enter_context(nc.sbuf_tensor("kv_acc", [128, 8 * 128], F32))
    qf32 = ab1.enter_context(nc.sbuf_tensor("qf32", [128, 8 * 512], F32))
    qsq16_0 = ab1.enter_context(nc.sbuf_tensor("qsq16_0", [128, 512], F16))
    qsq16_1 = ab1.enter_context(nc.sbuf_tensor("qsq16_1", [128, 512], F16))
    qsq16 = [qsq16_0, qsq16_1]
    k16_0 = ab1.enter_context(nc.sbuf_tensor("k16_0", [128, FD], F16))
    k16 = [k16_0, k16_0]
    v16_0 = ab1.enter_context(nc.sbuf_tensor("v16_0", [128, FD], F16))
    v16 = [v16_0, v16_0]
    ksq = ab1.enter_context(nc.sbuf_tensor("ksq", [128, FD], F16))
    t1b = ab2.enter_context(nc.sbuf_tensor("t1b", [128, FD], F32, side="right"))
    rtb = ab2.enter_context(nc.sbuf_tensor("rtb_s", [128, FD], F16, side="right"))
    q16 = ab2.enter_context(nc.sbuf_tensor("q16", [128, 8 * 2048], F16, side="right"))
    kv16 = ab2.enter_context(nc.sbuf_tensor("kv16", [128, 8 * 128], F16, side="right"))
    attn16 = ab2.enter_context(nc.sbuf_tensor("attn16", [128, 8 * 512], F16, side="right"))
    ao_sb0 = ab2.enter_context(nc.sbuf_tensor("ao_sb0", [128, FD], F16, side="right"))
    ao_sb = [ao_sb0, ao_sb0]

    def g_dma(dst, src, wait=(), slow=False, inc="DG"):
        return p.emit("gpsimd",
                      lambda e, d=dst, s=src, sl=slow: e.dma_start(
                          out=d, in_=s, allow_slow_non_contiguous=sl),
                      wait=wait, inc=inc, n=16)

    def v_dma(dst, src, wait=(), slow=False, inc="DB"):
        return p.emit("scalar",
                      lambda e, d=dst, s=src, sl=slow: e.dma_start(
                          out=d, in_=s, allow_slow_non_contiguous=sl),
                      wait=wait, inc=inc, n=16)

    def s_dma(dst, src, wait=(), inc="DX0"):
        return p.emit("sync", lambda e, d=dst, s=src: e.dma_start(out=d, in_=s),
                      wait=wait, inc=inc, n=16)

    p.emit("vector", lambda e: e.memset(TL["r1"][:], 0.0), inc="Z")
    g_dma(c16[:], dc16[:], inc="DC")
    g_dma(idF32[:], dc16[:, 0:128], inc="DC")
    g_dma(hmT16[:], dhmT[:], inc="DC")
    tok_rtb = g_dma(rtb[:], drt[:], inc="DC")
    tok_c = tok_rtb
    tok_m = v_dma(m16[:], dm.rearrange("(h o) -> h o", o=1), slow=True)
    v_dma(bq_c[:], dB["bq"].rearrange("(m p) -> p m", p=128), slow=True)
    v_dma(b1l_c[:], dB["b1l"].rearrange("(m p) -> p m", p=128), slow=True)
    v_dma(b1h_c[:], dB["b1h"].rearrange("(m p) -> p m", p=128), slow=True)
    v_dma(gl_c[:], dB["gl"].rearrange("(m p) -> p m", p=128), slow=True)
    v_dma(gh_c[:], dB["gh"].rearrange("(m p) -> p m", p=128), slow=True)
    for nm in ("bk", "bv", "bo", "b2l", "b2h"):
        g_dma(brow[nm], dB[nm].rearrange("(o f) -> o f", o=1), inc="DB")
    tok_b = ("DB", p.semv["DB"])
    wsem = {"Wq": "WQ", "Wk": "WK", "Wv": "WV", "Wo": "WO"}
    tok_w = {}
    for nm in ("Wq", "Wk", "Wv", "Wo"):
        for k in range(8):
            tok_w[nm] = g_dma(W16[nm][:, 1024 * k:1024 * (k + 1)],
                              dW[nm][128 * k:128 * (k + 1), :], inc=wsem[nm])

    t = p.emit("scalar", lambda e: e.activation(CL["t1"][0:16, :], m16[:], AF.Exp, scale=-1.0),
               wait=[tok_m], inc="S")
    t = p.emit("vector", lambda e: e.tensor_scalar_add(CL["t2"][0:16, :], CL["t1"][0:16, :], 1.0),
               wait=[t], inc="V")
    t = p.emit("vector", lambda e: e.reciprocal(CL["t1"][0:16, :], CL["t2"][0:16, :]),
               wait=[t], inc="V")
    t = p.emit("scalar", lambda e: e.activation(CL["t2"][0:16, :], CL["t1"][0:16, :],
                                                AF.Exp, scale=LN2048),
               wait=[t], inc="S")
    tok_rs = p.emit("vector", lambda e: e.reciprocal(rscale[:], CL["t2"][0:16, :]),
                    wait=[t], inc="V")
    tok_srt = p.emit("scalar", lambda e: e.activation(junkS[:], rtb[:], AF.Copy, accum_out=sumrt[:]),
                     wait=[tok_rtb], inc="S")

    war = {}

    def ln_stats_inv(src_tile, wait_toks):
        wts = list(wait_toks) + [war.get("lnstats")]
        t_s = p.emit("scalar", lambda e: e.activation(junkS[:], src_tile[:], AF.Square,
                                                      accum_out=TL["r1"][:]),
                     wait=wts, inc="S")
        t_v = p.emit("vector", lambda e: e.reduce_sum(TL["r2"][:], src_tile[:], AX.X),
                     wait=wts, inc="V")
        t_m = p.emit("vector", lambda e: e.tensor_scalar_mul(TL["r3"][:], TL["r2"][:], 1.0 / FD),
                     wait=[t_v], inc="V")
        t_a = p.emit("vector", lambda e: e.tensor_scalar(TL["r4"][:], TL["r3"][:], TL["r3"][:],
                                                         -1e-5, ALU.mult, ALU.add),
                     wait=[t_m], inc="V")
        t_b = p.emit("vector", lambda e: e.scalar_tensor_tensor(TL["r4"][:], TL["r1"][:], 1.0 / FD,
                                                                TL["r4"][:], ALU.mult, ALU.subtract),
                     wait=[t_a, t_s], inc="V")
        t_l = p.emit("scalar", lambda e: e.activation(TL["r5"][:], TL["r4"][:], AF.Ln),
                     wait=[t_b], inc="S")
        t_i = p.emit("scalar", lambda e: e.activation(TL["r6"][:], TL["r5"][:], AF.Exp, scale=-0.5),
                     wait=[t_l], inc="S")
        return t_i

    def phase_A(b, pump):
        tok_kv_stopped = None
        def xload(tc):
            buf = tc % 2
            war[("ax", tc)] = s_dma(xin[buf][:], dx[b, 128 * tc:128 * (tc + 1), :],
                                    wait=[war.get(("xin", buf))], inc=f"DX{buf}")

        xload(0)
        xload(1)
        for st4 in range(4):
            ev_last = None
            for i in range(4):
                tc = 4 * st4 + i
                buf = tc % 2
                tok_x = war.get(("ax", tc))
                t_inv = ln_stats_inv(xin[buf], [tok_x])
                t_xn = p.emit("vector",
                              lambda e, buf=buf: e.tensor_scalar(
                                  xn16[buf][:], xin[buf][:], TL["r3"][:], TL["r6"][:],
                                  ALU.subtract, ALU.mult),
                              wait=[t_inv, war.get(("xn16", buf))], inc="V")
                war[("xin", buf)] = t_xn
                war["lnstats"] = t_xn
                if tc + 2 < NT:
                    xload(tc + 2)
                tt = None
                for kc in range(8):
                    tt = p.emit("tensor",
                                lambda e, buf=buf, kc=kc: e.transpose(
                                    pool16[:, 128 * kc:128 * (kc + 1)],
                                    xn16[buf][:, 128 * kc:128 * (kc + 1)], id16),
                                wait=[t_xn, war.get("pool16"), tok_c], inc="T")
                war[("xn16", buf)] = tt
                ev = p.emit("scalar",
                            lambda e, i=i: e.copy(
                                xnT[:].rearrange("p (k t) -> p k t", k=8)[:, :, 128 * i:128 * (i + 1)],
                                pool16[:].rearrange("p (k t) -> p k t", k=8)),
                            wait=[tt, war.get("xnT")], inc="S")
                war["pool16"] = ev
                ev_last = ev
            tok_xnT = ev_last

            tok_q_ev = [None] * 8
            tok_hs = None
            for mq in range(8):
                half = mq % 2
                psl = poolA[:, 512 * half:512 * (half + 1)]
                tk = None
                for kc in range(8):
                    tk = p.emit("tensor",
                                lambda e, mq=mq, kc=kc, psl=psl: e.matmul(
                                    psl,
                                    W16["Wq"][:, 1024 * kc + 128 * mq:1024 * kc + 128 * (mq + 1)],
                                    xnT[:, 512 * kc:512 * (kc + 1)],
                                    start=(kc == 0), stop=(kc == 7)),
                                wait=[tok_xnT, war.get(("poolA", half)), tok_w["Wq"]], inc="T")
                ev = p.emit("scalar",
                            lambda e, mq=mq, psl=psl: e.copy(
                                qf32[:, 512 * mq:512 * (mq + 1)], psl),
                            wait=[tk], inc="S")
                war[("poolA", half)] = ev
                tok_q_ev[mq] = ev
                sq = p.emit("scalar",
                            lambda e, mq=mq: e.activation(
                                qsq16[mq % 2][:], qf32[:, 512 * mq:512 * (mq + 1)], AF.Square,
                                bias=bq_c[:, mq:mq + 1]),
                            wait=[ev, tok_b, war.get(("qsq", mq % 2))], inc="S")
                tok_hs = p.emit("tensor",
                                lambda e, mq=mq: e.matmul(
                                    poolD[0:16, 0:512], c16[:, 384 + 16 * mq:384 + 16 * (mq + 1)],
                                    qsq16[mq % 2][:], start=(mq == 0), stop=(mq == 7)),
                                wait=[sq] + ([war.get("poolD")] if mq == 0 else []), inc="T")
                war[("qsq", mq % 2)] = tok_hs
            t0 = p.emit("vector", lambda e: e.tensor_scalar_max(qinv[:], poolD[0:16, 0:512], 1e-24),
                        wait=[tok_hs], inc="V")
            war["poolD"] = t0
            t1_ = p.emit("scalar", lambda e: e.activation(qinv[:], qinv[:], AF.Ln),
                         wait=[t0], inc="S")
            t2_ = p.emit("scalar", lambda e: e.activation(qinv[:], qinv[:], AF.Exp, scale=-0.5),
                         wait=[t1_], inc="S")
            t3_ = p.emit("vector", lambda e: e.tensor_scalar(qinv16[:], qinv[:], rscale[:], None,
                                                             ALU.mult),
                         wait=[t2_, tok_rs, war.get("qinv16")], inc="V")
            tok_ap = [t3_, t3_]
            last_bc = None
            for mq in range(8):
                half = mq % 2
                psl = poolA[:, 512 * half:512 * (half + 1)]
                bc = p.emit("tensor",
                            lambda e, mq=mq, psl=psl: e.matmul(
                                psl, hmT16[:, 128 * mq:128 * (mq + 1)], qinv16[:],
                                start=True, stop=True),
                            wait=[tok_ap[half], tok_q_ev[7], t3_], inc="T")
                ap_ = p.emit("vector",
                             lambda e, mq=mq, st4=st4, psl=psl: e.scalar_tensor_tensor(
                                 q16[:, 2048 * mq + 512 * st4:2048 * mq + 512 * (st4 + 1)],
                                 qf32[:, 512 * mq:512 * (mq + 1)], bq_c[:, mq:mq + 1], psl,
                                 ALU.add, ALU.mult),
                             wait=[bc], inc="V")
                tok_ap[half] = ap_
                last_bc = bc
            war[("poolA", 0)] = tok_ap[0]
            war[("poolA", 1)] = tok_ap[1]
            war["qinv16"] = last_bc
            tok_qdone = tok_ap

            xnT3 = xnT[:].rearrange("p (k t) -> p k t", k=8)
            tkv = None
            for i in range(4):
                tc = 4 * st4 + i
                buf = tc % 2
                tk = None
                for kc in range(8):
                    for nh in range(2):
                        tk = p.emit("tensor",
                                    lambda e, kc=kc, nh=nh, i=i: e.matmul(
                                        poolA[:, 512 * nh:512 * (nh + 1)],
                                        xnT3[:, kc, 128 * i:128 * (i + 1)],
                                        W16["Wk"][:, 1024 * kc + 512 * nh:1024 * kc + 512 * (nh + 1)],
                                        start=(kc == 0), stop=False),
                                    wait=[tok_qdone[0], tok_qdone[1], tok_w["Wk"], tok_b,
                                          war.get(("poolA", nh))] if kc == 0 else [], inc="T")
                for nh in range(2):
                    tk = p.emit("tensor",
                                lambda e, nh=nh: e.matmul(
                                    poolA[:, 512 * nh:512 * (nh + 1)], ones_for["bk"],
                                    brow["bk"][:, 512 * nh:512 * (nh + 1)],
                                    start=False, stop=True),
                                wait=(), inc="T")
                sq = p.emit("scalar", lambda e: e.activation(ksq[:], poolA[:], AF.Square),
                            wait=[tk, war.get("ksq")], inc="S")
                rd = p.emit("vector", lambda e: e.tensor_reduce(
                    kstat[:], ksq[:].rearrange("p (h d) -> p h d", h=16), AX.X, ALU.add),
                            wait=[sq, war.get("kstat")], inc="V")
                war["ksq"] = rd
                mx_ = p.emit("vector", lambda e: e.tensor_scalar_max(kstat[:], kstat[:], 1e-24),
                             wait=[rd], inc="V")
                l1 = p.emit("scalar", lambda e: e.activation(kstat[:], kstat[:], AF.Ln),
                            wait=[mx_], inc="S")
                l2 = p.emit("scalar", lambda e: e.activation(kstat[:], kstat[:], AF.Exp, scale=-0.5),
                            wait=[l1], inc="S")
                ap_ = p.emit("vector",
                             lambda e, buf=buf: e.tensor_tensor(
                                 k16[buf][:].rearrange("p (h d) -> p h d", h=16),
                                 poolA[:].rearrange("p (h d) -> p h d", h=16),
                                 kstat[:].rearrange("p (h o) -> p h o", o=1).broadcast_to((128, 16, 64)),
                                 ALU.mult),
                             wait=[l2, war.get(("k16", 0))], inc="V")
                war["kstat"] = ap_
                war[("poolA", 0)] = ap_
                war[("poolA", 1)] = ap_
                tv = None
                for kc in range(8):
                    for nh in range(2):
                        tv = p.emit("tensor",
                                    lambda e, kc=kc, nh=nh, i=i: e.matmul(
                                        poolB[:, 512 * nh:512 * (nh + 1)],
                                        xnT3[:, kc, 128 * i:128 * (i + 1)],
                                        W16["Wv"][:, 1024 * kc + 512 * nh:1024 * kc + 512 * (nh + 1)],
                                        start=(kc == 0), stop=False),
                                    wait=[tok_w["Wv"], war.get(("poolB", nh))] if kc == 0 else [],
                                    inc="T")
                for nh in range(2):
                    tv = p.emit("tensor",
                                lambda e, nh=nh: e.matmul(
                                    poolB[:, 512 * nh:512 * (nh + 1)], ones_for["bv"],
                                    brow["bv"][:, 512 * nh:512 * (nh + 1)],
                                    start=False, stop=True),
                                wait=(), inc="T")
                ev = p.emit("scalar", lambda e, buf=buf: e.copy(v16[buf][:], poolB[:]),
                            wait=[tv, war.get(("v16", 0))], inc="S")
                war[("poolB", 0)] = ev
                war[("poolB", 1)] = ev
                first = (tc == 0)
                last = (tc == NT - 1)
                if first:
                    tz_kv = p.emit("vector", lambda e: e.memset(kv_acc[:], 0.0),
                                   wait=[war.get("kv_acc")], inc="V")
                    war[("kv_acc", "z")] = tz_kv
                for pr in range(8):
                    tkv = p.emit("tensor",
                                 lambda e, pr=pr, buf=buf: e.matmul(
                                     poolK[:, 128 * pr:128 * (pr + 1)],
                                     k16[buf][:, 128 * pr:128 * (pr + 1)],
                                     v16[buf][:, 128 * pr:128 * (pr + 1)],
                                     start=True, stop=True),
                                 wait=([ap_, ev] + ([war.get("poolK")] if first else
                                                    [war.get(("poolK", "acc"))])), inc="T")
                war[("k16", 0)] = tkv
                war[("v16", 0)] = tkv
                tacc = p.emit("vector",
                              lambda e: e.tensor_tensor(kv_acc[:], kv_acc[:], poolK[:], ALU.add),
                              wait=[tkv, war.get(("kv_acc", "z"))], inc="V")
                war[("poolK", "acc")] = tacc
                war["kv_acc"] = tacc
                if last:
                    tok_kv_stopped = tacc
                pump(3)
            war["xnT"] = tkv
            pump(2)

        tz = p.emit("vector", lambda e: e.memset(kv16[:], 0.0),
                    wait=[war.get("kv16")], inc="V")
        evl = None
        for pr in range(8):
            e1 = p.emit("scalar",
                        lambda e, pr=pr: e.copy(kv16[0:64, 128 * pr:128 * pr + 64],
                                                kv_acc[0:64, 128 * pr:128 * pr + 64]),
                        wait=[tok_kv_stopped, tz], inc="S")
            evl = p.emit("scalar",
                         lambda e, pr=pr: e.copy(kv16[64:128, 128 * pr + 64:128 * (pr + 1)],
                                                 kv_acc[64:128, 128 * pr + 64:128 * (pr + 1)]),
                         wait=[e1], inc="S")
        war["poolK"] = war.get(("poolK", "acc"))
        war["kv_acc"] = evl
        return evl

    def phase_B(b, tok_kv, pump):
        if ddbg2 is not None and b == 0:
            p.emit("gpsimd", lambda e: e.dma_start(
                out=ddbg2[5][:, 0:1024], in_=kv16[:].rearrange("p (a f) -> p a f", a=1)[:, 0, :]),
                   wait=[tok_kv], inc="DG", n=16)
        for st4 in range(4):
            attn_ev = [None] * 8
            for mq in range(8):
                half = mq % 2
                psl = poolA[:, 512 * half:512 * (half + 1)]
                mm = p.emit("tensor",
                            lambda e, mq=mq, st4=st4, psl=psl: e.matmul(
                                psl, kv16[:, 128 * mq:128 * (mq + 1)],
                                q16[:, 2048 * mq + 512 * st4:2048 * mq + 512 * (st4 + 1)],
                                start=True, stop=True),
                            wait=[tok_kv, war.get(("poolA", half))], inc="T")
                ev = p.emit("scalar",
                            lambda e, mq=mq, psl=psl: e.copy(
                                attn16[:, 512 * mq:512 * (mq + 1)], psl),
                            wait=[mm, war.get("attn16")], inc="S")
                war[("poolA", half)] = ev
                attn_ev[mq] = ev

            last_tk = None
            for i in range(4):
                tc = 4 * st4 + i
                buf = tc % 2
                pp = poolB if buf == 0 else poolK
                ppk = "poolWo0" if buf == 0 else "poolWo1"
                extra = []
                if b == 0 and st4 == 0 and i < 2:
                    extra = [tok_kv, war.get(("poolB", 0)), war.get(("poolB", 1))]
                tk = None
                for kc in range(8):
                    for nh in range(2):
                        tk = p.emit("tensor",
                                    lambda e, kc=kc, nh=nh, i=i, pp=pp: e.matmul(
                                        pp[:, 512 * nh:512 * (nh + 1)],
                                        attn16[:, 512 * kc + 128 * i:512 * kc + 128 * (i + 1)],
                                        W16["Wo"][:, 1024 * kc + 512 * nh:1024 * kc + 512 * (nh + 1)],
                                        start=(kc == 0), stop=False),
                                    wait=[attn_ev[kc]] +
                                         (([tok_w["Wo"], war.get(ppk)] + extra)
                                          if kc == 0 else []) if nh == 0 else [],
                                    inc="T")
                for nh in range(2):
                    tk = p.emit("tensor",
                                lambda e, nh=nh, pp=pp: e.matmul(
                                    pp[:, 512 * nh:512 * (nh + 1)], ones_for["bo"],
                                    brow["bo"][:, 512 * nh:512 * (nh + 1)],
                                    start=False, stop=True),
                                wait=(), inc="T")
                last_tk = tk
                c1 = p.emit("scalar",
                            lambda e, buf=buf, pp=pp: e.activation(
                                ao_sb[buf][:], pp[:], AF.Copy, accum_out=TL["r1"][:]),
                            wait=[tk, war.get(("ao", 0)), war.get("lnstats")], inc="S")
                if ddbg2 is not None and b == 0 and tc == 0:
                    p.emit("gpsimd", lambda e: e.dma_start(out=ddbg2[4], in_=ao_sb[0][:]),
                           wait=[c1], inc="DG", n=16)
                c2 = p.emit("scalar",
                            lambda e, pp=pp: e.activation(junkS[:], pp[:], AF.Square,
                                                          accum_out=TL["r2"][:]),
                            wait=[tk, war.get("lnstats")], inc="S")
                war[ppk] = c2
                t_m = p.emit("vector", lambda e: e.tensor_scalar_mul(TL["r3"][:], TL["r1"][:], 1.0 / FD),
                             wait=[c1], inc="V")
                t_a = p.emit("vector", lambda e: e.tensor_scalar(TL["r4"][:], TL["r3"][:], TL["r3"][:],
                                                                 -1e-5, ALU.mult, ALU.add),
                             wait=[t_m], inc="V")
                t_b = p.emit("vector", lambda e: e.scalar_tensor_tensor(
                    TL["r4"][:], TL["r2"][:], 1.0 / FD, TL["r4"][:], ALU.mult, ALU.subtract),
                             wait=[t_a, c2], inc="V")
                t_l = p.emit("scalar", lambda e: e.activation(TL["r5"][:], TL["r4"][:], AF.Ln),
                             wait=[t_b], inc="S")
                t_i = p.emit("scalar", lambda e: e.activation(TL["r6"][:], TL["r5"][:], AF.Exp,
                                                              scale=-0.5),
                             wait=[t_l], inc="S")
                tok_xr = s_dma(xrs[buf][:], dx[b, 128 * tc:128 * (tc + 1), :],
                               wait=[war.get(("xrs", buf))], inc=f"XR{buf}")
                tt1 = p.emit("vector",
                             lambda e, buf=buf: e.tensor_scalar(
                                 t1b[:], ao_sb[buf][:], TL["r3"][:], TL["r6"][:],
                                 ALU.subtract, ALU.mult),
                             wait=[t_i, war.get("t1b")], inc="V")
                war[("ao", 0)] = tt1
                war["lnstats"] = tt1
                tx1 = p.emit("vector",
                             lambda e, buf=buf, tc=tc, b=b: e.scalar_tensor_tensor(
                                 f_sb[buf][:], t1b[:], 0.0, xrs[buf][:], ALU.add, ALU.add,
                                 accum_out=ST[b]["sum3"][:, tc:tc + 1]),
                             wait=[tt1, tok_xr, war.get(("f_sb", buf)),
                                   war.get(("f_sb", buf, "s")), war.get(("f_sb", buf, "d"))],
                             inc="V")
                war["t1b"] = tx1
                war[("xrs", buf)] = tx1
                td = s_dma(dx1[b, 128 * tc:128 * (tc + 1), :], f_sb[buf][:],
                           wait=[tx1], inc=f"DS{buf}")
                ts_ = p.emit("scalar",
                             lambda e, buf=buf, tc=tc, b=b: e.activation(
                                 ao_sb[buf][:], f_sb[buf][:], AF.Square,
                                 accum_out=ST[b]["sumsq3"][:, tc:tc + 1]),
                             wait=[tx1], inc="S")
                tr_ = p.emit("vector",
                             lambda e, buf=buf, tc=tc, b=b: e.scalar_tensor_tensor(
                                 ao_sb[buf][:], f_sb[buf][:], 0.0, rtb[:], ALU.add, ALU.mult,
                                 accum_out=ST[b]["xr"][:, tc:tc + 1]),
                             wait=[tx1, tok_rtb, ts_], inc="V")
                war[("ao", 0)] = tr_
                war[("f_sb", buf)] = tr_
                war[("f_sb", buf, "s")] = ts_
                war[("f_sb", buf, "d")] = td
                pump(3)
            war["attn16"] = last_tk
            pump(2)
        tok_ds_end = (("DS0", p.semv.get("DS0", 0)), ("DS1", p.semv.get("DS1", 0)))
        sb_ = ST[b]
        t_m3 = p.emit("vector", lambda e: e.tensor_scalar_mul(sb_["m3"][:], sb_["sum3"][:], 1.0 / FD),
                      inc="V")
        t_a = p.emit("vector", lambda e: e.tensor_tensor(RT["t_t"][:], sb_["m3"][:], sb_["m3"][:],
                                                         ALU.mult),
                     wait=[t_m3], inc="V")
        t_v = p.emit("vector", lambda e: e.scalar_tensor_tensor(
            RT["t_t"][:], sb_["sumsq3"][:], 1.0 / FD, RT["t_t"][:], ALU.mult, ALU.subtract),
                     wait=[t_a], inc="V")
        t_ve = p.emit("vector", lambda e: e.tensor_scalar_add(RT["e_t"][:], RT["t_t"][:], 1e-5),
                      wait=[t_v], inc="V")
        t_l = p.emit("scalar", lambda e: e.activation(RT["e_t"][:], RT["e_t"][:], AF.Ln),
                     wait=[t_ve], inc="S")
        t_i3 = p.emit("scalar", lambda e: e.activation(RT["e_t"][:], RT["e_t"][:], AF.Exp, scale=-0.5),
                      wait=[t_l], inc="S")
        t_mx = p.emit("vector", lambda e: e.tensor_scalar_max(RT["w_t"][:], RT["t_t"][:], 1e-30),
                      wait=[t_v], inc="V")
        t_l2 = p.emit("scalar", lambda e: e.activation(RT["w_t"][:], RT["w_t"][:], AF.Ln),
                      wait=[t_mx], inc="S")
        t_sq = p.emit("scalar", lambda e: e.activation(RT["w_t"][:], RT["w_t"][:], AF.Exp, scale=0.5),
                      wait=[t_l2], inc="S")
        t_nx = p.emit("vector", lambda e: e.scalar_tensor_tensor(
            RT["w_t"][:], RT["w_t"][:], 32.0, RT["e_t"][:], ALU.mult, ALU.mult),
                      wait=[t_sq, t_i3], inc="V")
        t_nc = p.emit("vector", lambda e: e.tensor_scalar_max(RT["w_t"][:], RT["w_t"][:], 1e-12),
                      wait=[t_nx], inc="V")
        t_rc = p.emit("vector", lambda e: e.reciprocal(RT["w_t"][:], RT["w_t"][:]),
                      wait=[t_nc], inc="V")
        t_al = p.emit("vector", lambda e: e.scalar_tensor_tensor(
            sb_["alpha"][:], RT["w_t"][:], 32.0, RT["e_t"][:], ALU.mult, ALU.mult),
                      wait=[t_rc], inc="V")
        t_s1 = p.emit("vector", lambda e: e.tensor_scalar(
            RT["t_t"][:], sb_["m3"][:], sumrt[:], None, ALU.mult),
                      wait=[t_m3, tok_srt], inc="V")
        t_s2 = p.emit("vector", lambda e: e.tensor_tensor(RT["t_t"][:], sb_["xr"][:], RT["t_t"][:],
                                                          ALU.subtract),
                      wait=[t_s1], inc="V")
        t_s3 = p.emit("vector", lambda e: e.tensor_tensor(sb_["s_t"][:], RT["t_t"][:], RT["e_t"][:],
                                                          ALU.mult),
                      wait=[t_s2, t_al], inc="V")
        return t_s3, tok_ds_end

    I32 = mybir.dt.int32

    SHUF = {sh: [min(i + sh, 31) for i in range(32)] for sh in (8, 4, 2, 1)}
    BC0 = [0] * 32

    def build_routing(b, tok_s, g):
        sb_ = ST[b]
        steps = []
        state = {}

        def qsum(red_tok, out_name):
            t1 = p.emit("vector", lambda e: e.transpose(g["qt2"][:], g["qt1"][:]),
                        wait=[red_tok], inc="V")
            t2 = p.emit("vector", lambda e: e.reduce_sum(g["tr2"], g["qt2"][:], AX.X),
                        wait=[t1], inc="V")
            t3 = p.emit("vector", lambda e, out_name=out_name: e.stream_shuffle(
                g[out_name], g["tr2"], BC0),
                        wait=[t2], inc="V")
            war[("rtr", b)] = t3
            return t3

        def qtree(red_tok, op, fill, out_name):
            t0 = p.emit("vector", lambda e: e.memset(g["tr1"], fill),
                        wait=[war.get(("rtr", b))], inc="V")
            t1 = p.emit("vector", lambda e: e.tensor_copy(g["tr1"][0:16, :], g["redc"][0:16, :]),
                        wait=[t0, red_tok], inc="V")
            for sh in (8, 4, 2, 1):
                t2 = p.emit("vector", lambda e, sh=sh: e.stream_shuffle(
                    g["tr2"], g["tr1"], SHUF[sh]),
                            wait=[t1], inc="V")
                t1 = p.emit("vector", lambda e: e.tensor_tensor(
                    g["tr1"][0:16, :], g["tr1"][0:16, :], g["tr2"][0:16, :], op),
                            wait=[t2], inc="V")
            t3 = p.emit("vector", lambda e, out_name=out_name: e.stream_shuffle(
                g[out_name], g["tr1"], BC0),
                        wait=[t1], inc="V")
            war[("rtr", b)] = t3
            return t3

        def init():
            tt = p.emit("tensor", lambda e: e.transpose(poolD[0:16, 0:128], sb_["s_t"][:],
                                                        idF32[:]),
                        wait=[tok_s, war.get("poolD"), tok_c], inc="T")
            tv = p.emit("vector", lambda e: e.tensor_copy(g["s16"][0:16, :], poolD[0:16, 0:128]),
                        wait=[tt], inc="V")
            war["poolD"] = tv
            t1 = p.emit("vector", lambda e: e.tensor_scalar_mul(g["sp16"][0:16, :],
                                                                g["s16"][0:16, :], 10.0),
                        wait=[tv], inc="V")
            t2 = p.emit("scalar", lambda e: e.activation(g["bp16"][0:16, :], g["sp16"][0:16, :],
                                                         AF.Relu),
                        wait=[t1], inc="S")
            t3 = p.emit("vector", lambda e: e.reduce_max(g["redc"][0:16, :], g["sp16"][0:16, :],
                                                         AX.X),
                        wait=[t1], inc="V")
            t4 = qtree(t3, ALU.max, -1e30, "nsmaxs")
            t5 = p.emit("vector", lambda e: e.tensor_scalar_mul(g["nsmaxs"], g["nsmaxs"],
                                                                -1.0),
                        wait=[t4], inc="V")
            t6 = p.emit("vector", lambda e: e.memset(g["a16"], 0.0), inc="V")
            t6b = p.emit("vector", lambda e: e.memset(g["qt1"][:], 0.0), inc="V")
            state["qz"] = t6b
            t7 = p.emit("vector", lambda e: e.memset(g["a10s"], 0.0), inc="V")
            state["t"] = (t2, t5, t6, t7)

        steps.append(init)

        def coor_iter(it):
            def fn():
                tprev = state["t"]
                t1 = p.emit("vector", lambda e: e.tensor_tensor(g["t16"][0:16, :], g["sp16"][0:16, :],
                                                                g["bp16"][0:16, :], ALU.subtract),
                            wait=list(tprev), inc="V")
                t2 = p.emit("vector", lambda e: e.tensor_tensor(g["nmxs"], g["a10s"],
                                                                g["nsmaxs"], ALU.max),
                            wait=list(tprev), inc="V")
                t3 = p.emit("scalar", lambda e: e.activation(g["e16"][0:16, :], g["t16"][0:16, :],
                                                             AF.Exp, bias=g["nmxs"][0:16, :],
                                                             accum_out=g["qt1"][0:16, 0:1]),
                            wait=[t1, t2, state.get("qz")], inc="S")
                t5 = qsum(t3, "trb")
                t6 = p.emit("scalar", lambda e: e.activation(g["lns"], g["trb"], AF.Ln),
                            wait=[t5], inc="S")
                t7 = p.emit("vector", lambda e: e.tensor_scalar(g["tmps"], g["lns"], -0.1,
                                                                CONST_R, ALU.mult, ALU.add),
                            wait=[t6], inc="V")
                t8 = p.emit("vector", lambda e: e.scalar_tensor_tensor(
                    g["a16"], g["nmxs"], 0.1, g["tmps"], ALU.mult, ALU.add),
                            wait=[t7], inc="V")
                t9 = p.emit("vector", lambda e: e.tensor_scalar_mul(g["a10s"], g["a16"], 10.0),
                            wait=[t8], inc="V")
                t10 = p.emit("scalar", lambda e: e.activation(g["bp16"][0:16, :], g["sp16"][0:16, :],
                                                              AF.Relu, bias=g["a10s"][0:16, :]),
                             wait=[t9], inc="S")
                state["t"] = (t10,)
            return fn

        for it in range(COOR_ITERS):
            steps.append(coor_iter(it))

        def keys():
            tprev = state["t"]
            t1 = p.emit("vector", lambda e: e.tensor_scalar(
                g["w16"][0:16, :], g["s16"][0:16, :], g["a16"][0:16, :], 0.0, ALU.add, ALU.min),
                        wait=list(tprev), inc="V")
            t2a = p.emit("vector", lambda e: e.tensor_scalar(
                g["km16"][0:16, :], g["w16"][0:16, :], 0.2, -CUT_W * 0.2, ALU.mult, ALU.add),
                         wait=[t1], inc="V")
            t2 = p.emit("scalar", lambda e: e.activation(g["km16"][0:16, :], g["km16"][0:16, :],
                                                         AF.Exp),
                        wait=[t2a], inc="S")
            t3 = p.emit("vector", lambda e: e.tensor_scalar(g["km16"][0:16, :], g["km16"][0:16, :],
                                                            0.5, -2e7, ALU.mult, ALU.add),
                        wait=[t2], inc="V")
            t4 = p.emit("vector", lambda e: e.tensor_scalar(g["gti16"][0:16, :], g["w16"][0:16, :],
                                                            float(CUT_W), None, ALU.is_gt),
                        wait=[t1], inc="V")
            t5a = p.emit("vector", lambda e: e.tensor_copy(g["key16"][0:16, :], g["km16"][0:16, :]),
                         wait=[t3], inc="V")
            t5 = p.emit("vector", lambda e: e.copy_predicated(g["key16"][0:16, :], g["gti16"][0:16, :],
                                                              g["w16"][0:16, :]),
                        wait=[t5a, t4], inc="V")
            t6 = p.emit("vector", lambda e: e.memset(g["los"], -2e7), inc="V")
            t7 = p.emit("vector", lambda e: e.memset(g["his"], 1.0), inc="V")
            state["t"] = (t5, t6, t7)

        steps.append(keys)

        def search_iter(it):
            def fn():
                tprev = state["t"]
                t1 = p.emit("vector", lambda e: e.tensor_scalar(
                    g["mids"], g["los"], g["his"], 0.5, ALU.add, ALU.mult),
                            wait=list(tprev), inc="V")
                t2 = p.emit("vector", lambda e: e.tensor_scalar(
                    g["e16"][0:16, :], g["key16"][0:16, :], g["mids"][0:16, :], None, ALU.is_ge),
                            wait=[t1], inc="V")
                t3 = p.emit("vector", lambda e: e.reduce_sum(g["qt1"][0:16, 0:1], g["e16"][0:16, :],
                                                             AX.X),
                            wait=[t2], inc="V")
                t4 = qsum(t3, "cnts")
                t5 = p.emit("vector", lambda e: e.tensor_scalar(
                    g["pr16"], g["cnts"], 1024.0, None, ALU.is_ge),
                            wait=[t4], inc="V")
                t5b = p.emit("vector", lambda e: e.tensor_scalar(
                    g["np16"], g["cnts"], 1024.0, None, ALU.is_lt),
                             wait=[t4], inc="V")
                t6 = p.emit("vector", lambda e: e.copy_predicated(g["los"], g["pr16"],
                                                                  g["mids"]),
                            wait=[t5], inc="V")
                t7 = p.emit("vector", lambda e: e.copy_predicated(g["his"], g["np16"],
                                                                  g["mids"]),
                            wait=[t5b, t6], inc="V")
                state["t"] = (t6, t7)
            return fn

        for it in range(SEARCH_ITERS):
            steps.append(search_iter(it))

        def tail():
            tprev = state["t"]
            t1 = p.emit("vector", lambda e: e.tensor_scalar(
                g["ii16"][0:16, :], g["key16"][0:16, :], g["los"][0:16, :], None, ALU.is_ge),
                        wait=list(tprev), inc="V")
            t2a = p.emit("vector", lambda e: e.memset(g["km16"][0:16, :], 1e30),
                         wait=[t1], inc="V")
            t2 = p.emit("vector", lambda e: e.copy_predicated(g["km16"][0:16, :], g["ii16"][0:16, :],
                                                              g["key16"][0:16, :]),
                        wait=[t2a], inc="V")
            t3 = p.emit("vector", lambda e: e.tensor_reduce(g["redc"][0:16, :], g["km16"][0:16, :],
                                                            AX.X, ALU.min),
                        wait=[t2], inc="V")
            t4 = qtree(t3, ALU.min, 1e30, "taus")
            t5 = p.emit("vector", lambda e: e.tensor_scalar(
                g["e16"][0:16, :], g["key16"][0:16, :], g["taus"][0:16, :], None, ALU.is_gt),
                        wait=[t4], inc="V")
            t6 = p.emit("vector", lambda e: e.reduce_sum(g["qt1"][0:16, 0:1], g["e16"][0:16, :],
                                                         AX.X),
                        wait=[t5], inc="V")
            t7 = qsum(t6, "mgts")
            t8 = p.emit("vector", lambda e: e.tensor_scalar(
                g["qts"], g["mgts"], -1.0, 1024.0, ALU.mult, ALU.add),
                        wait=[t7], inc="V")
            t9 = p.emit("vector", lambda e: e.tensor_copy(
                g["rowbuf"], g["taus"][0:1, 0:1].broadcast_to((1, 128))),
                        wait=[t4], inc="V")
            t10 = p.emit("tensor", lambda e: e.transpose(poolD[:, 48:49], g["rowbuf"],
                                                         idF32[0:1, 0:1]),
                         wait=[t9, war.get("poolD")], inc="T")
            t11 = p.emit("vector", lambda e: e.tensor_copy(CL["tau"][:], poolD[:, 48:49]),
                         wait=[t10], inc="V")
            t12 = p.emit("vector", lambda e: e.tensor_copy(
                g["rowbuf"], g["qts"][0:1, 0:1].broadcast_to((1, 128))),
                         wait=[t8, t10], inc="V")
            t13 = p.emit("tensor", lambda e: e.transpose(poolD[:, 49:50], g["rowbuf"],
                                                         idF32[0:1, 0:1]),
                         wait=[t12], inc="T")
            t14 = p.emit("vector", lambda e: e.tensor_copy(CL["quota"][:], poolD[:, 49:50]),
                         wait=[t13], inc="V")
            t15 = p.emit("tensor", lambda e: e.transpose(poolD[:, 32:48], g["key16"][0:16, :],
                                                         idF32[0:16, 0:16]),
                         wait=[t2, t13], inc="T")
            t16 = p.emit("vector", lambda e: e.tensor_copy(RT["key"][:], poolD[:, 32:48]),
                         wait=[t15], inc="V")
            t17 = p.emit("vector", lambda e: e.tensor_scalar(
                RT["gtm"][:], RT["key"][:], CL["tau"][:], None, ALU.is_gt),
                         wait=[t16, t11], inc="V")
            t18 = p.emit("vector", lambda e: e.tensor_scalar(
                RT["eq"][:], RT["key"][:], CL["tau"][:], None, ALU.is_equal),
                         wait=[t16, t11], inc="V")
            t19 = p.emit("vector", lambda e: e.tensor_copy(eq16[:], RT["eq"][:]),
                         wait=[t18], inc="V")
            t20 = p.emit("tensor", lambda e: e.matmul(poolD[:, 0:16], ones128, eq16[:],
                                                      start=True, stop=True),
                         wait=[t19, t16], inc="T")
            t21 = p.emit("vector", lambda e: e.tensor_copy(RT["colsum"][:], poolD[:, 0:16]),
                         wait=[t20], inc="V")
            t22 = p.emit("tensor", lambda e: e.matmul(poolD[:, 16:32], tri16, eq16[:],
                                                      start=True, stop=True),
                         wait=[t19, t21], inc="T")
            t23 = p.emit("vector", lambda e: e.memset(RT["prefx"][:, 0:1], 0.0),
                         wait=[t21], inc="V")
            for c in range(1, NT):
                t23 = p.emit("vector", lambda e, c=c: e.tensor_tensor(
                    RT["prefx"][:, c:c + 1], RT["prefx"][:, c - 1:c], RT["colsum"][:, c - 1:c],
                    ALU.add),
                             wait=[t23], inc="V")
            t24 = p.emit("vector", lambda e: e.tensor_tensor(RT["cum"][:], poolD[:, 16:32],
                                                             RT["prefx"][:], ALU.add),
                         wait=[t22, t23], inc="V")
            t25 = p.emit("vector", lambda e: e.tensor_scalar(
                RT["cum"][:], RT["cum"][:], CL["quota"][:], None, ALU.is_le),
                         wait=[t24, t14], inc="V")
            t26 = p.emit("vector", lambda e: e.tensor_tensor(RT["cum"][:], RT["cum"][:], RT["eq"][:],
                                                             ALU.mult),
                         wait=[t25], inc="V")
            t27 = p.emit("vector", lambda e: e.tensor_tensor(sb_["mask"][:], RT["gtm"][:],
                                                             RT["cum"][:], ALU.add),
                         wait=[t26], inc="V")
            war["poolD"] = t27
            state["mask"] = t27
            if ddbg is not None:
                tc1 = p.emit("vector", lambda e: e.tensor_copy(
                    RT["t_t"][:], CL["tau"][:].broadcast_to((128, NT))),
                             wait=[t27], inc="V")
                s_dma(ddbg[b, 0], sb_["s_t"][:], wait=[t27], inc="DS0")
                s_dma(ddbg[b, 1], RT["gtm"][:], wait=[t27], inc="DS0")
                s_dma(ddbg[b, 2], RT["key"][:], wait=[t27], inc="DS0")
                s_dma(ddbg[b, 3], sb_["mask"][:], wait=[t27], inc="DS0")
                s_dma(ddbg[b, 4], sb_["m3"][:], wait=[t27], inc="DS0")
                s_dma(ddbg[b, 5], sb_["alpha"][:], wait=[t27], inc="DS0")
                s_dma(ddbg[b, 6], RT["t_t"][:], wait=[tc1], inc="DS0")
                s_dma(ddbg[b, 7], sb_["xr"][:], wait=[t27], inc="DS0")

        steps.append(tail)
        return steps, state

    cstack = ExitStack()
    c2stack = ExitStack()
    wstack = ExitStack()

    def load_w1_stage(barrierA):
        w1l16 = cstack.enter_context(nc.sbuf_tensor("w1l16", [128, 8 * 512], F16))
        w1h16 = cstack.enter_context(nc.sbuf_tensor("w1h16", [128, 8 * 4096], F16))
        hh16 = cstack.enter_context(nc.sbuf_tensor("hh16", [128, 4 * 256], F16))
        hl16 = cstack.enter_context(nc.sbuf_tensor("hl16", [128, 4 * 256], F16))
        wtmp0 = wstack.enter_context(nc.sbuf_tensor("wtmp0", [128, 2048], F32))
        wtmp = [wtmp0, wtmp0]
        C = dict(w1l16=w1l16, w1h16=w1h16, hh16=hh16, hl16=hl16)
        wsteps = []
        state = {"w1l": None, "w1h": None}

        def w1l_step(q):
            def fn():
                buf = q % 2
                td = p.emit("gpsimd", lambda e: e.dma_start(
                    out=wtmp[buf][:].rearrange("p (k f) -> p k f", k=4),
                    in_=dW["w1l"].rearrange("(k p) f -> p k f", p=128)[:, 4 * q:4 * (q + 1), :]),
                    wait=[war.get(("wtmp", 0)), barrierA], inc="DW", n=16)
                t = None
                for j in range(4):
                    kc = 4 * q + j
                    t = p.emit("scalar",
                               lambda e, kc=kc, j=j: e.activation(
                                   w1l16[:, 512 * kc:512 * (kc + 1)], wtmp[buf][:, 512 * j:512 * (j + 1)],
                                   AF.Copy, scale=gl_c[:, kc:kc + 1]),
                               wait=[td], inc="S")
                war[("wtmp", 0)] = t
                state["w1l"] = t
            return fn

        def w1h_step(kc, hf):
            def fn():
                buf = (2 * kc + hf) % 2
                td = p.emit("gpsimd", lambda e: e.dma_start(
                    out=wtmp[buf][:], in_=dW["w1h"][128 * kc:128 * (kc + 1), 2048 * hf:2048 * (hf + 1)]),
                    wait=[war.get(("wtmp", 0)), barrierA], inc="DW", n=16)
                ts_ = p.emit("scalar",
                             lambda e: e.activation(
                                 w1h16[:, 4096 * kc + 2048 * hf:4096 * kc + 2048 * (hf + 1)],
                                 wtmp[buf][:], AF.Copy, scale=gh_c[:, kc:kc + 1]),
                             wait=[td], inc="S")
                war[("wtmp", 0)] = ts_
                state["w1h"] = ts_
            return fn

        for q in range(2):
            wsteps.append(w1l_step(q))
        for kc in range(8):
            for hf in range(2):
                wsteps.append(w1h_step(kc, hf))
        C["_wstate"] = state
        return C, wsteps

    def load_w2_stage(C, barrierB):
        w2l16 = c2stack.enter_context(nc.sbuf_tensor("w2l16", [128, 4 * 1024], F16, side="right"))
        w2h16 = c2stack.enter_context(nc.sbuf_tensor("w2h16", [128, 32 * 1024], F16, side="right"))
        zt16 = [c2stack.enter_context(nc.sbuf_tensor(f"zt16_{i}", [128, 8 * 256], F16, side="right"))
                for i in range(2)]
        z16_c = c2stack.enter_context(nc.sbuf_tensor("z16_c", [128, FD], F16, side="right"))
        z16_c2 = c2stack.enter_context(nc.sbuf_tensor("z16_c2", [128, FD], F16, side="right"))
        z16 = [z16_c, z16_c2]
        C.update(w2l16=w2l16, w2h16=w2h16, zt16=zt16, z16=z16)
        for kc in range(4):
            p.emit("gpsimd", lambda e, kc=kc: e.dma_start(
                out=w2h16[:, 1024 * kc:1024 * (kc + 1)],
                in_=dW["w2h"][128 * kc:128 * (kc + 1), :]),
                wait=[barrierB], inc="DH0", n=16)
        tok_h0 = ("DH0", p.semv["DH0"])
        for kc in range(4):
            p.emit("gpsimd", lambda e, kc=kc: e.dma_start(
                out=w2l16[:, 1024 * kc:1024 * (kc + 1)],
                in_=dW["w2l"][128 * kc:128 * (kc + 1), :]),
                wait=[barrierB], inc="DL", n=16)
        tok_l = ("DL", p.semv["DL"])
        for kc in range(4, 16):
            p.emit("gpsimd", lambda e, kc=kc: e.dma_start(
                out=w2h16[:, 1024 * kc:1024 * (kc + 1)],
                in_=dW["w2h"][128 * kc:128 * (kc + 1), :]),
                wait=[barrierB], inc="DH1", n=16)
        tok_h1 = ("DH1", p.semv["DH1"])
        for kc in range(16, 32):
            p.emit("gpsimd", lambda e, kc=kc: e.dma_start(
                out=w2h16[:, 1024 * kc:1024 * (kc + 1)],
                in_=dW["w2h"][128 * kc:128 * (kc + 1), :]),
                wait=[barrierB], inc="DH2", n=16)
        tok_h2 = ("DH2", p.semv["DH2"])
        C["tok_w2l"] = tok_l
        C["tok_w2h"] = [tok_h0 if kc < 4 else tok_h1 if kc < 16 else tok_h2
                        for kc in range(32)]
        C["tok_w1l"] = C["_wstate"]["w1l"]
        C["tok_w1h"] = C["_wstate"]["w1h"]
        return C

    def phase_C(b, C, mask_getter, tok_ds_end, pump, first_c=False):
        sb_ = ST[b]
        pool16r = pool16[:].rearrange("p (k t) -> p k t", k=8)

        def zload(tc):
            buf = tc % 2
            tok = s_dma(xin[buf][:], dx1[b, 128 * tc:128 * (tc + 1), :],
                        wait=[war.get(("xin", buf))] +
                             (list(tok_ds_end) if tc < 2 else []), inc=f"DX{buf}")
            war[("xz", tc)] = tok

        def zprep(st2):
            par = st2 % 2
            zt = C["zt16"][par]
            zt3 = zt[:].rearrange("p (k t) -> p k t", k=8)
            ev_last = None
            for i in range(2):
                tc = 2 * st2 + i
                buf = tc % 2
                tz = p.emit("vector",
                            lambda e, buf=buf, tc=tc: e.tensor_scalar(
                                C["z16"][buf][:], xin[buf][:], sb_["m3"][:, tc:tc + 1],
                                sb_["alpha"][:, tc:tc + 1], ALU.subtract, ALU.mult),
                            wait=[war.get(("xz", tc)), war.get(("z16", buf))], inc="V")
                war[("xin", buf)] = tz
                tt = None
                for kc in range(8):
                    tt = p.emit("tensor",
                                lambda e, buf=buf, kc=kc: e.transpose(
                                    pool16[:, 128 * kc:128 * (kc + 1)],
                                    C["z16"][buf][:, 128 * kc:128 * (kc + 1)], id16),
                                wait=[tz, war.get("pool16")] +
                                     ([war.get("poolC_first")] if (first_c and st2 == 0) else []),
                                inc="T")
                war[("z16", buf)] = tt
                ev = p.emit("scalar",
                            lambda e, i=i, zt3=zt3: e.copy(zt3[:, :, 128 * i:128 * (i + 1)],
                                                           pool16r),
                            wait=[tt, war.get(("zt16", par))], inc="S")
                war["pool16"] = ev
                ev_last = ev
            war[("ztr", par)] = ev_last

        zload(0)
        zload(1)
        zprep(0)
        for st2 in range(8):
            par = st2 % 2
            tok_zt = war.get(("ztr", par))
            ztc = C["zt16"][par]
            if 2 * st2 + 2 < NT:
                zload(2 * st2 + 2)
                zload(2 * st2 + 3)
            for hh in range(8):
                ge_toks = [None] * 4
                for mh in range(4):
                    hcol = 4 * hh + mh
                    half = mh % 2
                    psl = poolK[:, 0:256] if half == 0 else poolK[:, 512:768]
                    tk = None
                    for kc in range(8):
                        tk = p.emit("tensor",
                                    lambda e, hcol=hcol, kc=kc, psl=psl, ztc=ztc: e.matmul(
                                        psl,
                                        C["w1h16"][:, 4096 * kc + 128 * hcol:4096 * kc + 128 * (hcol + 1)],
                                        ztc[:, 256 * kc:256 * (kc + 1)],
                                        start=(kc == 0), stop=(kc == 7)),
                                    wait=[tok_zt, war.get(("poolK", half)), C["tok_w1h"]]
                                    if kc == 0 else [], inc="T")
                    ge = p.emit("scalar",
                                lambda e, hcol=hcol, mh=mh, psl=psl: e.activation(
                                    C["hh16"][:, 256 * mh:256 * (mh + 1)],
                                    psl, AF.Gelu, bias=b1h_c[:, hcol:hcol + 1]),
                                wait=[tk, war.get(("hh16", mh))], inc="S")
                    war[("poolK", half)] = ge
                    ge_toks[mh] = ge
                for i in range(2):
                    tc = 2 * st2 + i
                    pp = poolA if i == 0 else poolB
                    ppn = "poolA" if i == 0 else "poolB"
                    tk = None
                    for kc in range(4):
                        hcol = 4 * hh + kc
                        for nh in range(2):
                            w_ = [ge_toks[kc], C["tok_w2h"][hcol] if (first_c and st2 == 0) else None]
                            if hh == 0 and kc == 0 and nh == 0:
                                w_ += [war.get((ppn, "c"))]
                                if first_c and st2 == 0:
                                    w_ += [war.get(("poolC", ppn))]
                            tk = p.emit("tensor",
                                        lambda e, kc=kc, hcol=hcol, nh=nh, i=i, pp=pp, hh=hh: e.matmul(
                                            pp[:, 512 * nh:512 * (nh + 1)],
                                            C["hh16"][:, 256 * kc + 128 * i:256 * kc + 128 * (i + 1)],
                                            C["w2h16"][:, 1024 * hcol + 512 * nh:1024 * hcol + 512 * (nh + 1)],
                                            start=(hh == 0 and kc == 0), stop=False),
                                        wait=w_, inc="T")
                    if hh == 7:
                        for nh in range(2):
                            tk = p.emit("tensor",
                                        lambda e, nh=nh, pp=pp: e.matmul(
                                            pp[:, 512 * nh:512 * (nh + 1)], ones_for["b2h"],
                                            brow["b2h"][:, 512 * nh:512 * (nh + 1)],
                                            start=False, stop=True),
                                        wait=(), inc="T")
                        war[(ppn, "h")] = tk
                    for kc in range(4):
                        war[("hh16", kc)] = tk
                pump(2)
                if hh == 3 and st2 < 7:
                    zprep(st2 + 1)
            pump(6)
            tok_hv = [None, None]
            for i in range(2):
                tc = 2 * st2 + i
                pp = poolA if i == 0 else poolB
                ppn = "poolA" if i == 0 else "poolB"
                tok_hv[i] = p.emit("vector",
                                   lambda e, i=i, tc=tc, pp=pp: e.tensor_scalar(
                                       hvm[i][:], pp[:], sb_["mask"][:, tc:tc + 1], None,
                                       ALU.mult),
                                   wait=[war.get((ppn, "h")), mask_getter(), war.get(("hvm", i))],
                                   inc="V")
            if ddbg2 is not None and b == 0 and st2 == 0:
                p.emit("gpsimd", lambda e: e.dma_start(out=ddbg2[3], in_=C["zt16"][0][:, 0:1024]),
                       wait=[tok_zt], inc="DG", n=16)
            lge = [None] * 4
            for ml in range(4):
                half = ml % 2
                psl = poolK[:, 0:256] if half == 0 else poolK[:, 512:768]
                tk = None
                for kc in range(8):
                    tk = p.emit("tensor",
                                lambda e, ml=ml, kc=kc, psl=psl, ztc=ztc: e.matmul(
                                    psl,
                                    C["w1l16"][:, 512 * kc + 128 * ml:512 * kc + 128 * (ml + 1)],
                                    ztc[:, 256 * kc:256 * (kc + 1)],
                                    start=(kc == 0), stop=(kc == 7)),
                                wait=[war.get(("poolK", half)), C["tok_w1l"],
                                      C["tok_w2l"]] if kc == 0 else [],
                                inc="T")
                ge = p.emit("scalar",
                            lambda e, ml=ml, psl=psl: e.activation(
                                C["hl16"][:, 256 * ml:256 * (ml + 1)], psl, AF.Gelu,
                                bias=b1l_c[:, ml:ml + 1]),
                            wait=[tk, war.get(("hl16", ml))], inc="S")
                war[("poolK", half)] = ge
                lge[ml] = ge
            war[("zt16", par)] = tk
            for i in range(2):
                tc = 2 * st2 + i
                buf = tc % 2
                pp = poolA if i == 0 else poolB
                ppn = "poolA" if i == 0 else "poolB"
                tk = None
                for kc in range(4):
                    for nh in range(2):
                        w_ = [lge[kc]]
                        if kc == 0 and nh == 0:
                            w_ += [tok_hv[i]]
                        tk = p.emit("tensor",
                                    lambda e, kc=kc, nh=nh, i=i, pp=pp: e.matmul(
                                        pp[:, 512 * nh:512 * (nh + 1)],
                                        C["hl16"][:, 256 * kc + 128 * i:256 * kc + 128 * (i + 1)],
                                        C["w2l16"][:, 1024 * kc + 512 * nh:1024 * kc + 512 * (nh + 1)],
                                        start=(kc == 0), stop=False),
                                    wait=w_, inc="T")
                for nh in range(2):
                    tk = p.emit("tensor",
                                lambda e, nh=nh, pp=pp: e.matmul(
                                    pp[:, 512 * nh:512 * (nh + 1)], ones_for["b2l"],
                                    brow["b2l"][:, 512 * nh:512 * (nh + 1)],
                                    start=False, stop=True),
                                wait=(), inc="T")
                for kc in range(4):
                    war[("hl16", kc)] = tk
                if ddbg2 is not None and b == 0 and st2 == 0 and i == 0:
                    p.emit("gpsimd", lambda e: e.dma_start(out=ddbg2[0], in_=C["z16"][0][:]),
                           wait=[war.get(("z16", 0)), tok_hv[0]], inc="DG", n=16)
                    p.emit("gpsimd", lambda e: e.dma_start(out=ddbg2[1], in_=hvm[0][:]),
                           wait=[tok_hv[0], tk], inc="DG", n=16)
                tf = p.emit("vector",
                            lambda e, i=i, tc=tc, pp=pp: e.scalar_tensor_tensor(
                                f_sb[i][:], pp[:], 0.0, hvm[i][:], ALU.add, ALU.add,
                                accum_out=sb_["sum4"][:, tc:tc + 1]),
                            wait=[tk, tok_hv[i], war.get(("f_sb", i)),
                                  war.get(("f_sb", i, "s")), war.get(("f_sb", i, "d")),
                                  war.get(("f_sb", i, "o"))], inc="V")
                war[(ppn, "c")] = tf
                war[("hvm", i)] = tf
                if ddbg2 is not None and b == 0 and st2 == 0 and i == 0:
                    p.emit("gpsimd", lambda e: e.dma_start(out=ddbg2[2], in_=f_sb[0][:]),
                           wait=[tf], inc="DG", n=16)
                ts_ = p.emit("scalar",
                             lambda e, i=i, tc=tc: e.activation(
                                 junkS[:], f_sb[i][:], AF.Square,
                                 accum_out=sb_["sumsq4"][:, tc:tc + 1]),
                             wait=[tf], inc="S")
                t_m = p.emit("vector", lambda e, tc=tc: e.tensor_scalar_mul(
                    T4["m4"][:], sb_["sum4"][:, tc:tc + 1], 1.0 / FD),
                             wait=[tf, war.get("t4")], inc="V")
                t_a = p.emit("vector", lambda e: e.tensor_scalar(T4["v4"][:], T4["m4"][:],
                                                                 T4["m4"][:], -1e-5, ALU.mult, ALU.add),
                             wait=[t_m], inc="V")
                t_b = p.emit("vector", lambda e, tc=tc: e.scalar_tensor_tensor(
                    T4["v4"][:], sb_["sumsq4"][:, tc:tc + 1], 1.0 / FD, T4["v4"][:],
                    ALU.mult, ALU.subtract),
                             wait=[t_a, ts_], inc="V")
                t_l = p.emit("scalar", lambda e: e.activation(T4["w4"][:], T4["v4"][:], AF.Ln),
                             wait=[t_b], inc="S")
                t_i = p.emit("scalar", lambda e: e.activation(T4["i4"][:], T4["w4"][:], AF.Exp,
                                                              scale=-0.5),
                             wait=[t_l], inc="S")
                tok_xr = s_dma(xrs[i][:], dx1[b, 128 * tc:128 * (tc + 1), :],
                               wait=[war.get(("xrs", i))] +
                                    (list(tok_ds_end) if st2 == 0 else []), inc=f"XR{i}")
                tt1 = p.emit("vector",
                             lambda e, i=i: e.tensor_scalar(
                                 hvm[i][:], f_sb[i][:], T4["m4"][:], T4["i4"][:],
                                 ALU.subtract, ALU.mult),
                             wait=[t_i, war.get(("hvm", i))], inc="V")
                war[("f_sb", i)] = tt1
                war["t4"] = tt1
                to_ = p.emit("vector",
                             lambda e, i=i: e.tensor_tensor(f_sb[i][:], hvm[i][:], xrs[i][:], ALU.add),
                             wait=[tt1, tok_xr, ts_, war.get(("f_sb", i, "o")),
                                   war.get(("f_sb", i, "d"))], inc="V")
                war[("hvm", i)] = to_
                war[("xrs", i)] = to_
                td = s_dma(dout[b, 128 * tc:128 * (tc + 1), :], f_sb[i][:], wait=[to_],
                           inc=f"DO{i}")
                war[("f_sb", i, "o")] = to_
                war[("f_sb", i, "d")] = td

    routing_q = []

    def pump(k):
        for _ in range(min(k, len(routing_q))):
            routing_q.pop(0)()

    def pump_all():
        while routing_q:
            routing_q.pop(0)()

    def nopump(k):
        return None

    tok_kv0 = phase_A(0, nopump)
    tok_s0, tok_ds0 = phase_B(0, tok_kv0, nopump)
    steps0, rstate0 = build_routing(0, tok_s0, RB0)
    routing_q.extend(steps0)
    tok_kv1 = phase_A(1, pump)
    barrierA = [("T", p.semv.get("T", 0)), ("V", p.semv.get("V", 0)),
                ("S", p.semv.get("S", 0))]
    ab1.close()
    C, wsteps = load_w1_stage(barrierA[0])
    p.wait("gpsimd", *barrierA[1:])
    merged = []
    ri = wi = 0
    while wi < len(wsteps) or ri < len(routing_q):
        if wi < len(wsteps):
            merged.append(wsteps[wi]); wi += 1
        if ri < len(routing_q):
            merged.append(routing_q[ri]); ri += 1
        if wi < len(wsteps):
            merged.append(wsteps[wi]); wi += 1
    routing_q[:] = merged
    tok_s1, tok_ds1 = phase_B(1, tok_kv1, pump)

    barrierB = [("T", p.semv.get("T", 0)), ("V", p.semv.get("V", 0)),
                ("S", p.semv.get("S", 0))]
    war[("poolC", "poolA")] = barrierB[1]
    war[("poolC", "poolB")] = barrierB[2]
    war["poolC_first"] = barrierB[2]
    ab2.close()
    wstack.close()
    p.wait("gpsimd", *barrierB)
    C = load_w2_stage(C, barrierB[0])

    steps1, rstate1 = build_routing(1, tok_s1, RB0)
    routing_q.extend(steps1)
    phase_C(0, C, lambda: rstate0["mask"], tok_ds0, pump, first_c=True)
    pump_all()
    phase_C(1, C, lambda: rstate1["mask"], tok_ds1, nopump)

    for s in ("DO0", "DO1", "DS0", "DS1"):
        if s in p.semv:
            p.wait("sync", (s, p.semv[s]))

    with nc.Block() as block:
        @block.sync
        def _(e):
            for f in p.seq["sync"]:
                f(e)

        @block.gpsimd
        def _(e):
            for f in p.seq["gpsimd"]:
                f(e)

        @block.tensor
        def _(e):
            for f in p.seq["tensor"]:
                f(e)

        @block.vector
        def _(e):
            for f in p.seq["vector"]:
                f(e)

        @block.scalar
        def _(e):
            for f in p.seq["scalar"]:
                f(e)

    cstack.close()
    c2stack.close()
    return nc, stack


def make_consts():
    c = np.zeros((128, 512), np.float32)
    c[:, 0:128] = np.eye(128, dtype=np.float32)
    c[:, 128:256] = np.triu(np.ones((128, 128), np.float32))
    c[:, 256:384] = 1.0
    for m in range(8):
        for ptn in range(128):
            h = 2 * m + ptn // 64
            c[ptn, 384 + 16 * m + h] = 1.0
    hmT = np.zeros((16, 8 * 128), np.float32)
    for m in range(8):
        for ptn in range(128):
            hmT[2 * m + ptn // 64, 128 * m + ptn] = 1.0
    return c, hmT


_PROG_CACHE = {}


def kernel(L0_attn_x, L0_attn_Wq, L0_attn_bq, L0_attn_Wk, L0_attn_bk, L0_attn_Wv,
           L0_attn_bv, L0_attn_Wo, L0_attn_bo, L0_attn_m, L0_ffn_routing_token,
           L0_ffn_gamma_light, L0_ffn_w1_light, L0_ffn_b1_light, L0_ffn_w2_light,
           L0_ffn_b2_light, L0_ffn_gamma_heavy, L0_ffn_w1_heavy, L0_ffn_b1_heavy,
           L0_ffn_w2_heavy, L0_ffn_b2_heavy, _debug=False):
    key = ("prog", _debug)
    if key not in _PROG_CACHE:
        _PROG_CACHE[key] = build_program(debug=_debug)
    nc, _stack = _PROG_CACHE[key]
    cc, hmT = make_consts()
    x = np.ascontiguousarray(np.asarray(L0_attn_x, dtype=np.float32))
    rtbv = np.ascontiguousarray(np.broadcast_to(
        np.asarray(L0_ffn_routing_token, np.float32)[None, :], (128, FD)))
    base = {
        "Wq": np.ascontiguousarray(L0_attn_Wq), "Wk": np.ascontiguousarray(L0_attn_Wk),
        "Wv": np.ascontiguousarray(L0_attn_Wv), "Wo": np.ascontiguousarray(L0_attn_Wo),
        "w1l": np.ascontiguousarray(L0_ffn_w1_light), "w2l": np.ascontiguousarray(L0_ffn_w2_light),
        "w1h": np.ascontiguousarray(L0_ffn_w1_heavy), "w2h": np.ascontiguousarray(L0_ffn_w2_heavy),
        "bq": np.ascontiguousarray(L0_attn_bq), "bk": np.ascontiguousarray(L0_attn_bk),
        "bv": np.ascontiguousarray(L0_attn_bv), "bo": np.ascontiguousarray(L0_attn_bo),
        "b1l": np.ascontiguousarray(L0_ffn_b1_light), "b2l": np.ascontiguousarray(L0_ffn_b2_light),
        "b1h": np.ascontiguousarray(L0_ffn_b1_heavy), "b2h": np.ascontiguousarray(L0_ffn_b2_heavy),
        "gl": np.ascontiguousarray(L0_ffn_gamma_light), "gh": np.ascontiguousarray(L0_ffn_gamma_heavy),
        "mh": np.ascontiguousarray(L0_attn_m), "rtb": rtbv, "c16": cc, "hmT": hmT,
    }
    in_maps = []
    for c in range(8):
        d = dict(base)
        d["x"] = x[NB * c:NB * (c + 1)]
        in_maps.append(d)
    res = run_bass_kernel_spmd(nc, in_maps, core_ids=list(range(8)))
    out = np.concatenate([np.asarray(r["out"]) for r in res.results], axis=0)
    if _debug:
        dbg = np.stack([np.asarray(r["dbg"]) for r in res.results], axis=0)
        dbg2 = np.stack([np.asarray(r["dbg2"]) for r in res.results], axis=0)
        np.save('/tmp/dev_dbg2.npy', dbg2)
        x1s = np.stack([np.asarray(r["x1s"]) for r in res.results], axis=0)
        np.save('/tmp/dev_x1s.npy', x1s)
        return out.astype(np.float32, copy=False), dbg
    return out.astype(np.float32, copy=False)
